# version 15
# baseline (speedup 1.0000x reference)
import os
import numpy as np
import ml_dtypes
from contextlib import ExitStack

import concourse.bass as bass
import concourse.mybir as mybir
from concourse.bass_utils import run_bass_kernel_spmd

F32 = mybir.dt.float32
BF16 = mybir.dt.bfloat16
Alu = mybir.AluOpType
Act = mybir.ActivationFunctionType
AX = mybir.AxisListType

N_CORES = 8
D = 1024
NH = 16
C = 64
E = 8
HF = 2048
DCH = D // 128
HCH = HF // 128
LN_EPS = 1e-5
ALPHA = 0.5
TB = 512
TSN = TB // 128
FCONST_VALS = (1e-5, 1.0 / 1024, -0.5, 0.125, -1.0, 1.0, 0.0, 2.0)


def TT(out, a, b, op):
    return lambda e, out=out, a=a, b=b, op=op: e.tensor_tensor(out, a, b, op)

def STT(out, in0, s, in1, op0, op1):
    return lambda e, out=out, in0=in0, s=s, in1=in1, op0=op0, op1=op1: \
        e.scalar_tensor_tensor(out, in0, s, in1, op0, op1)

def TS(out, a, s, op):
    return lambda e, out=out, a=a, s=s, op=op: e.tensor_scalar(out, a, s, None, op)

def CPY(out, a):
    return lambda e, out=out, a=a: e.tensor_copy(out, a)

def ACTV(out, a, f, bias=0.0, scale=1.0):
    return lambda e, out=out, a=a, f=f, bias=bias, scale=scale: \
        e.activation(out, a, f, bias=bias, scale=scale)

def RED(out, a, op, neg=False, absv=False):
    return lambda e, out=out, a=a, op=op, neg=neg, absv=absv: \
        e.tensor_reduce(out, a, AX.X, op, negate=neg, apply_absolute_value=absv)

def AMR(out, accum, in0, in1, scale, bias):
    return lambda e, out=out, accum=accum, in0=in0, in1=in1, scale=scale, bias=bias: \
        e.affine_mul_reduce(out=out, accum_out=accum, in0=in0, in1=in1,
                            scale=scale, bias=bias)

def RECIP(out, a):
    return lambda e, out=out, a=a: e.reciprocal(out, a)

def MM(out, l, r, st, sp):
    return lambda e, out=out, l=l, r=r, st=st, sp=sp: \
        e.matmul(out, l, r, start=st, stop=sp)

def TRP(out, a, ident):
    return lambda e, out=out, a=a, ident=ident: e.transpose(out, a, ident)


class Tracker:
    CE = ("pe", "dve", "act", "gp")

    def __init__(self, nc, es):
        self.nc = nc
        self.stream = {e: [] for e in ("pe", "dve", "act", "gp", "sync")}
        self.sem = {e: es.enter_context(nc.semaphore(f"sem_{e}")) for e in self.CE}
        self.cnt = {e: 0 for e in self.CE}
        self.dma_sem = {}
        self.dma_cnt = {}
        self.last_w = {}
        self.readers = {}
        self.hwm = {}

    def new_dma_sem(self, es, name):
        self.dma_sem[name] = es.enter_context(self.nc.semaphore("dq_" + name))
        self.dma_cnt[name] = 0

    def _wait(self, we, src):
        kind, name, cnt = src
        if kind == "d":
            cnt = self.dma_cnt[name]
        key = (we, kind, name)
        if self.hwm.get(key, -1) >= cnt:
            return
        self.hwm[key] = cnt
        sem = self.sem[name] if kind == "e" else self.dma_sem[name]
        self.stream[we].append(lambda eng, sem=sem, cnt=cnt: eng.wait_ge(sem, cnt))

    def _pre(self, we, reads, writes):
        for r in reads:
            lw = self.last_w.get(r)
            if lw is not None:
                self._wait(we, lw)
            if we in ("act", "dve") and isinstance(r, tuple) and r and r[0] == "pb":
                other = "dve" if we == "act" else "act"
                for rd in self.readers.get(r, ()):
                    if rd[0] == "e" and rd[1] == other:
                        self._wait(we, rd)
        for w in writes:
            lw = self.last_w.get(w)
            if lw is not None:
                self._wait(we, lw)
            for rd in self.readers.get(w, ()):
                self._wait(we, rd)

    def _post(self, kind, name, cnt, reads, writes):
        src = (kind, name, cnt)
        for r in reads:
            self.readers.setdefault(r, []).append(src)
        for w in writes:
            self.last_w[w] = src
            self.readers[w] = []

    def op(self, we, emit, reads=(), writes=()):
        self.group(we, [emit], reads, writes)

    def group(self, we, emits, reads=(), writes=()):
        self._pre(we, reads, writes)
        self.cnt[we] += 1
        cnt = self.cnt[we]
        sem = self.sem[we]

        def run(eng, emits=tuple(emits), sem=sem):
            n = len(emits)
            for i, em in enumerate(emits):
                inst = em(eng)
                if i == n - 1:
                    inst.then_inc(sem, 1)

        self.stream[we].append(run)
        self._post("e", we, cnt, reads, writes)

    def dma(self, qe, sem_name, out, in_, reads=(), writes=()):
        self._pre(qe, reads, writes)
        self.dma_cnt[sem_name] += 16
        cnt = self.dma_cnt[sem_name]
        sem = self.dma_sem[sem_name]

        def run(eng, out=out, in_=in_, sem=sem):
            eng.dma_start(out, in_).then_inc(sem, 16)

        self.stream[qe].append(run)
        self._post("d", sem_name, cnt, reads, writes)

    def barrier(self):
        snap = dict(self.cnt)
        dsnap = dict(self.dma_cnt)
        for we in self.CE + ("sync",):
            for se in self.CE:
                if se != we and snap[se] > 0:
                    self._wait(we, ("e", se, snap[se]))
            for dn, c in dsnap.items():
                if c > 0:
                    self._wait(we, ("d", dn, c))


def build_program(TOK, dbg=False):
    MP = 2 if TOK >= 2048 else 1
    MT = TOK // MP
    NT = MT // 128
    NTB = MT // TB

    nc = bass.Bass("TRN2", target_bir_lowering=False)
    es = ExitStack()

    def din(name, shape, dt=F32):
        return nc.dram_tensor(name, list(shape), dt, kind="ExternalInput").ap()

    x_d = din("x", (TOK, D))
    wqkv_d = din("wqkv", (D, 3 * D))
    gw_d = din("gw", (D, E))
    gb_d = din("gb", (E, 1))
    wmoe_d = din("wmoe", (E, HCH, 128, 2048))
    wo_d = din("wo", (D, D))
    bqkvo_d = din("bqkvo", (4, D))
    b1p_d = din("b1p", (128, 128))
    b2_d = din("b2", (E, D))
    gamma_d = din("gamma", (1, D))
    beta_d = din("beta", (1, D))
    identf_d = din("identf", (128, 128))
    identb_d = din("identb", (128, 128), BF16)
    ones1b_d = din("ones1b", (1, 128), BF16)
    ones1f_d = din("ones1f", (1, 128))
    cmask_d = din("cmask", (128, 1024), BF16)
    sel4_d = din("sel4", (4, 512), BF16)
    fconst_d = din("fconst", (128, 8))

    out_d = nc.dram_tensor("out", [TOK, D], F32, kind="ExternalOutput").ap()
    at_d = nc.dram_tensor("atspill", [MP * NT, 128, D], BF16, kind="Internal").ap()

    dbg_d = {}
    if dbg:
        for nm, sh in (("xn_dbg", (TOK, D)), ("gate_dbg", (TOK, E)),
                       ("sel_dbg", (TOK, 4)), ("a_dbg", (TOK, D)),
                       ("eo1_dbg", (TOK, D)), ("z_dbg", (TOK, D)),
                       ("qkv_dbg", (3, 128, D)),):
            dbg_d[nm] = nc.dram_tensor(nm, list(sh), F32, kind="ExternalOutput").ap()

    sb = lambda n, s, d=F32: es.enter_context(nc.sbuf_tensor(n, list(s), d))

    xnT = sb("xnT", (128, DCH, MT), BF16)
    acc = sb("acc", (128, NT, D), F32)
    mT = sb("mT", (E, MT), BF16)
    v1s = sb("v1s", (128, NT), F32)
    v2s = sb("v2s", (128, NT), F32)
    oh1s = sb("oh1s", (128, NT, E), F32)
    w2sc = sb("w2sc", (128, NT, E), F32)
    b1s = sb("b1s", (128, 128), F32)
    b2hi = sb("b2hi", (E, D), BF16)
    b2lo = sb("b2lo", (E, D), BF16)
    gbs = sb("gbs", (E, 1), F32)
    bqkvo_s = sb("bqkvo_s", (4, D), BF16)
    sel4 = sb("sel4_s", (4, 512), BF16)
    identf = sb("identf_s", (128, 128), F32)
    identb = sb("identb_s", (128, 128), BF16)
    ones1b = sb("ones1b_s", (1, 128), BF16)
    ones1f = sb("ones1f_s", (1, 128), F32)
    cmask = sb("cmask_s", (128, 1024), BF16)
    gws = sb("gws", (128, DCH * E), F32)
    fconst = sb("fconst_s", (128, 8), F32)
    sW = sb("sW", (128, 3, 1024), BF16)
    qkvw = sb("qkvw", (128, 2, 2, 1024), BF16)
    qt = sb("qt", (128, 2, D), BF16)
    kt = sb("kt", (128, 2, D), BF16)
    vt = sb("vt", (128, 2, D), BF16)
    sS = sb("sS", (128, 3, 1024), F32)
    sE = sb("sE", (128, 3, 1024), F32)
    sA = sb("sA", (128, D), BF16)
    atstg = sb("atstg", (128, D), BF16)
    w1s = sb("w1s", (128, 2, 4, 1024), BF16)
    w2res = sb("w2res", (128, HCH, 1024), BF16)
    hstg = sb("hstg", (128, HCH, TB), BF16)
    xls = sb("xls", (128, 2, D), F32)
    scA = sb("scA", (128, 2, D), F32)
    scD = sb("scD", (128, 2, D), F32)
    scS = sb("scS", (128, 8), F32)
    atst = sb("atst", (128, 2, 32), F32)
    atst2 = sb("atst2", (128, 2, 16), F32)
    lnst = sb("lnst", (128, NT, 2), F32)
    dpart = sb("dpart", (128, C), F32)
    npart = sb("npart", (128, C), F32)
    rtmp = sb("rtmp", (128, 96), F32)
    g8a = sb("g8a", (E, 128), F32)
    g8b = sb("g8b", (128, E), F32)
    g8c = sb("g8c", (128, E), F32)
    g8d = sb("g8d", (128, E), F32)
    mrow = sb("mrow", (128, 2, E), F32)
    dbgbuf = sb("dbgbuf", (128, D), F32) if dbg else None
    wos = w2res[:].rearrange("p a b -> p (a b)")[:, 0:DCH * D].rearrange(
        "p (c f) -> p c f", c=DCH)
    atld = hstg[:].rearrange("p a b -> p (a b)")[:, 0:2 * D].rearrange(
        "p (c f) -> p c f", c=2)
    grep = sS[:].rearrange("p a b -> p (a b)")[:, 0:D]
    brep = sS[:].rearrange("p a b -> p (a b)")[:, D:2 * D]
    b2f = scA[0:E, 0, :]

    pb = [es.enter_context(nc.psum_tensor(f"pb{i}", [128, 512], F32)) for i in range(8)]
    PBANK = lambda i: ("pb", i)

    tr = Tracker(nc, es)
    for s in (["xin0", "xin1", "wq0", "wq1", "cst", "cstg",
               "out", "at", "dbg"] + [f"w1s{i}" for i in range(4)]
              + [f"w2c{i}" for i in range(2)]):
        tr.new_dma_sem(es, s)

    x_t = x_d.rearrange("(t p) d -> t p d", p=128)
    out_t = out_d.rearrange("(t p) d -> t p d", p=128)

    def dbg_dump(dst, src_ap, reads):
        w = src_ap.shape[-1]
        tr.op("act", ACTV(dbgbuf[:, 0:w], src_ap, Act.Copy),
              reads=list(reads), writes=[("dbgB",)])
        tr.dma("sync", "dbg", dst, dbgbuf[:, 0:w], reads=[("dbgB",)])

    for dst, src, key in ((identf[:], identf_d, "identf"), (identb[:], identb_d, "identb"),
                          (ones1b[:], ones1b_d, "ones"), (ones1f[:], ones1f_d, "onesf"),
                          (b1s[:], b1p_d, "b1s"), (gbs[:], gb_d, "gbs")):
        tr.dma("sync", "cst", dst, src, writes=[(key,)])
    tr.dma("sync", "cst", gws[:].rearrange("p (c e) -> p c e", c=DCH),
           gw_d.rearrange("(c p) e -> p c e", p=128), writes=[("gws",)])
    tr.dma("gp", "cstg", cmask[:], cmask_d, writes=[("cmask",)])
    tr.dma("sync", "cst", fconst[:], fconst_d, writes=[("fconst",)])
    for _i, _v in enumerate(FCONST_VALS):
        nc.const_aps.register(F32, _v, fconst[:, _i:_i + 1])
    tr.dma("gp", "cstg", bqkvo_s[:], bqkvo_d, writes=[("bqkvo",)])
    tr.dma("sync", "cst", sel4[:], sel4_d, writes=[("sel4",)])
    tr.dma("sync", "cst", b2f, b2_d, writes=[("b2f",)])
    tr.op("act", ACTV(b2hi[:], b2f, Act.Copy), reads=[("b2f",)], writes=[("b2hi",)])
    tr.op("dve", TT(b2lo[:], b2f, b2hi[:], Alu.subtract),
          reads=[("b2f",), ("b2hi",)], writes=[("b2lo",)])
    tr.barrier()

    def ln_stats(xap, xregs, mu, rr):
        tr.op("dve", RED(mu, xap, Alu.add), reads=xregs, writes=[("mu",)])
        tr.op("dve", TS(mu, mu, -1.0 / D, Alu.mult), reads=[("mu",)], writes=[("mu",)])
        tr.op("dve", lambda e, xap=xap, mu=mu, rr=rr: e.scalar_tensor_tensor(
            scD[:, 0, :], xap, mu, xap, Alu.add, Alu.mult, accum_out=rr),
              reads=xregs + [("mu",)], writes=[("var",), ("body",)])
        tr.op("act", ACTV(rr, rr, Act.Ln, bias=LN_EPS, scale=1.0 / D),
              reads=[("var",)], writes=[("var",)])
        tr.op("act", ACTV(rr, rr, Act.Exp, scale=-0.5), reads=[("var",)], writes=[("var",)])
        tr.op("dve", TT(mu, mu, rr, Alu.mult), reads=[("mu",), ("var",)], writes=[("mu",)])

    P0LITE = os.environ.get("KR_P0LITE", "")
    def emit_p0_tile(mp, t):
        par = t % 2
        gt = mp * NT + t
        xreg = ("xls", par)
        tr.dma("sync", "xin0" if par == 0 else "xin1", xls[:, par, :], x_t[gt], writes=[xreg])
        mu, rr = lnst[:, t, 0:1], lnst[:, t, 1:2]
        ln_stats(xls[:, par, :], [xreg], mu, rr)
        tr.op("dve", CPY(scS[:, 0:2], lnst[:, t, :]),
              reads=[("mu",), ("var",)], writes=[("lnst", t)])
        xnreg = ("scA", par)
        tr.op("act", ACTV(scA[:, par, :], xls[:, par, :], Act.Identity, bias=mu, scale=rr),
              reads=[xreg, ("lnst", t)], writes=[xnreg])
        if dbg:
            dbg_dump(dbg_d["xn_dbg"].rearrange("(t p) d -> t p d", p=128)[gt],
                     scA[:, par, :], [xnreg])
        if P0LITE == "nope":
            return
        for dc in range(DCH):
            bk = 6 + (dc % 2)
            tr.group("pe", [TRP(pb[bk][:][:, 0:128],
                                scA[:, par, dc * 128:(dc + 1) * 128], identf[:])],
                     reads=[xnreg, ("identf",)], writes=[PBANK(bk)])
            tr.op("act", ACTV(xnT[:, dc, t * 128:(t + 1) * 128], pb[bk][:][:, 0:128],
                              Act.Copy),
                  reads=[PBANK(bk)], writes=[("xnT", dc, t)])
            xf = scD[:, 1, (dc % 2) * 128:(dc % 2) * 128 + 128]
            tr.op("act", ACTV(xf, pb[bk][:][:, 0:128], Act.Copy),
                  reads=[PBANK(bk)], writes=[("xf32", dc % 2)])
            tr.group("pe", [MM(pb[5][:][0:E, 0:128], gws[:, dc * E:(dc + 1) * E],
                               xf, dc == 0, dc == DCH - 1)],
                     reads=[("xf32", dc % 2), ("gws",)], writes=[PBANK(5)])
        if P0LITE in ("nosmall", "nogate"):
            return
        tr.op("act", ACTV(g8a[:], pb[5][:][0:E, 0:128], Act.Identity, bias=gbs[:]),
              reads=[PBANK(5), ("gbs",)], writes=[("g8a",)])
        tr.group("pe", [TRP(pb[4][:][:, 0:E], g8a[:], identf[0:E, 0:E])],
                 reads=[("g8a",), ("identf",)], writes=[PBANK(4)])
        tr.op("dve", CPY(g8b[:], pb[4][:][:, 0:E]), reads=[PBANK(4)], writes=[("g8b",)])
        if dbg:
            dbg_dump(dbg_d["gate_dbg"].rearrange("(t p) e -> t p e", p=128)[gt],
                     g8b[:], [("g8b",)])
        mg, sg = scS[:, 2:3], scS[:, 3:4]
        tr.op("dve", RED(mg, g8b[:], Alu.max, neg=True), reads=[("g8b",)], writes=[("mg",)])
        tr.op("act", ACTV(g8c[:], g8b[:], Act.Exp, bias=mg),
              reads=[("g8b",), ("mg",)], writes=[("g8c",)])
        tr.op("dve", RED(sg, g8c[:], Alu.add), reads=[("g8c",)], writes=[("sg",)])
        tr.op("dve", RECIP(sg, sg), reads=[("sg",)], writes=[("sg",)])
        tr.op("dve", TS(g8c[:], g8c[:], sg, Alu.mult), reads=[("g8c",), ("sg",)],
              writes=[("g8c",)])
        va, vb = v1s[:, t:t + 1], v2s[:, t:t + 1]
        tr.op("dve", RED(va, g8c[:], Alu.max), reads=[("g8c",)], writes=[("v1s", t)])
        tr.op("dve", TS(mrow[:, 0, :], g8c[:], va, Alu.is_equal),
              reads=[("g8c",), ("v1s", t)], writes=[("mrow", 0)])
        tr.op("dve", STT(g8d[:], mrow[:, 0, :], -1e30, g8c[:], Alu.mult, Alu.add),
              reads=[("mrow", 0), ("g8c",)], writes=[("g8d",)])
        tr.op("dve", RED(vb, g8d[:], Alu.max), reads=[("g8d",)], writes=[("v2s", t)])
        tr.op("dve", TS(mrow[:, 1, :], g8d[:], vb, Alu.is_equal),
              reads=[("g8d",), ("v2s", t)], writes=[("mrow", 1)])
        tr.op("dve", CPY(oh1s[:, t, :], mrow[:, 0, :]),
              reads=[("mrow", 0)], writes=[("oh1s", t)])
        tr.op("dve", TS(w2sc[:, t, :], mrow[:, 1, :], vb, Alu.mult),
              reads=[("mrow", 1), ("v2s", t)], writes=[("w2sc", t)])
        tr.group("pe", [TRP(pb[4][:][0:E, 0:128], mrow[:, 0, :], identf[:])],
                 reads=[("mrow", 0), ("identf",)], writes=[PBANK(4)])
        tr.op("act", ACTV(mT[:, t * 128:(t + 1) * 128], pb[4][:][0:E, 0:128],
                          Act.Copy),
              reads=[PBANK(4)], writes=[("mT", t)])
        if dbg:
            tr.op("dve", CPY(rtmp[:, 0:1], va), reads=[("v1s", t)], writes=[("seldmp",)])
            tr.op("dve", CPY(rtmp[:, 1:2], vb), reads=[("v2s", t)], writes=[("seldmp",)])
            tr.op("dve", RED(rtmp[:, 2:3], g8c[:], Alu.add), reads=[("g8c",)],
                  writes=[("seldmp",)])
            tr.op("dve", RED(rtmp[:, 3:4], g8d[:], Alu.max), reads=[("g8d",)],
                  writes=[("seldmp",)])
            dbg_dump(dbg_d["sel_dbg"].rearrange("(t p) e -> t p e", p=128)[gt],
                     rtmp[:, 0:4], [("seldmp",)])

    def emit_qkv_pair(mp, t0):
        tiles = [t0, t0 + 1] if t0 + 1 < NT else [t0]
        dsts = ((qt, "qt"), (kt, "kt"), (vt, "vt"))
        for proj in range(3):
            for qtr in range(4):
                sl = qtr % 2
                tr.dma("gp", f"wq{sl}", qkvw[:, sl, :, :],
                       wqkv_d[qtr * 256:(qtr + 1) * 256,
                              proj * D:(proj + 1) * D].rearrange(
                           "(c p) f -> p c f", p=128),
                       writes=[("qkvw", sl)])
                for ti, t in enumerate(tiles):
                    mms = []
                    for dq in range(2):
                        dc = qtr * 2 + dq
                        for nh in range(2):
                            wv = qkvw[:, sl, dq, nh * 512:(nh + 1) * 512]
                            mms.append(MM(pb[2 + 2 * ti + nh][:][:, 0:512],
                                          xnT[:, dc, t * 128:(t + 1) * 128], wv,
                                          dc == 0, False))
                    if qtr == 3:
                        for nh in range(2):
                            mms.append(MM(pb[2 + 2 * ti + nh][:][:, 0:512],
                                          sel4[:, proj * 128:(proj + 1) * 128],
                                          bqkvo_s[:, nh * 512:(nh + 1) * 512],
                                          False, True))
                    rds = [("xnT", qtr * 2 + dq, t) for dq in range(2)] + \
                          [("qkvw", sl), ("sel4",), ("bqkvo",)]
                    tr.group("pe", mms, reads=rds,
                             writes=[PBANK(2 + 2 * ti), PBANK(3 + 2 * ti)])
            for ti, t in enumerate(tiles):
                par = t % 2
                dst, key = dsts[proj]
                scl = 0.125 if proj == 0 else 1.0
                for nh in range(2):
                    tr.op("act", ACTV(dst[:, par, nh * 512:(nh + 1) * 512],
                                      pb[2 + 2 * ti + nh][:][:, 0:512], Act.Copy,
                                      scale=scl),
                          reads=[PBANK(2 + 2 * ti + nh)], writes=[(key, par)])
                if dbg and t == 0:
                    dbg_dump(dbg_d["qkv_dbg"][proj], dst[:, par, :], [(key, par)])
        for t in tiles:
            par = t % 2
            qv3 = qt[:, par, :].rearrange("p (h c) -> p h c", h=NH)
            kv3 = kt[:, par, :].rearrange("p (h c) -> p h c", h=NH)
            tr.op("dve", RED(atst[:, par, 0:16], qv3, Alu.max, absv=True),
                  reads=[("qt", par)], writes=[("qmax", par)])
            tr.op("dve", RED(atst[:, par, 16:32], kv3, Alu.max, absv=True, neg=True),
                  reads=[("kt", par)], writes=[("kmax", par)])
            tr.op("dve", TT(atst2[:, par, :], atst[:, par, 0:16], atst[:, par, 16:32],
                            Alu.mult),
                  reads=[("qmax", par), ("kmax", par)], writes=[("ngcs", par)])

    spc = [0]
    def emit_attn_iter(mp, t, h):
        par = t % 2
        qs = qt[:, par, h * C:(h + 1) * C]
        ks = kt[:, par, h * C:(h + 1) * C]
        vs = vt[:, par, h * C:(h + 1) * C]
        rq, rk, rv = ("qt", par), ("kt", par), ("vt", par)
        ngc = atst2[:, par, h:h + 1]
        rngc = ("ngcs", par)

        WID = {"B": 1024, 1: 256, 2: 512, 3: 768}
        base = spc[0]
        SP = {"B": base % 3, 1: (base + 1) % 3, 2: (base + 2) % 3, 3: (base + 3) % 3}
        spc[0] = base + 4

        def qkv_aps(tag):
            if tag == "B":
                qv = qs.rearrange("p (b q) -> p b q", b=4).unsqueeze(3) \
                    .broadcast_to([128, 4, 16, 16])
                kv = ks.rearrange("p (b k) -> p b k", b=4).unsqueeze(2) \
                    .broadcast_to([128, 4, 16, 16])
                vv = vs.rearrange("p (b k) -> p b k", b=4).unsqueeze(2) \
                    .broadcast_to([128, 4, 16, 16])
            else:
                j = tag
                w = 16 * j
                qv = qs[:, 16 * j:16 * j + 16].unsqueeze(2).broadcast_to([128, 16, w])
                kv = ks[:, 0:w].unsqueeze(1).broadcast_to([128, 16, w])
                vv = vs[:, 0:w].unsqueeze(1).broadcast_to([128, 16, w])
            return qv, kv, vv

        def sview(buf, sp, tag):
            w = WID[tag]
            flat = buf[:, sp, 0:w]
            if tag == "B":
                return flat.rearrange("p (b q k) -> p b q k", b=4, q=16), flat
            return flat.rearrange("p (q k) -> p q k", q=16), flat

        def smult(tag):
            sp = SP[tag]
            qv, kv, _ = qkv_aps(tag)
            o3, _ = sview(sS, sp, tag)
            tr.op("gp", TT(o3, qv, kv, Alu.mult), reads=[rq, rk],
                  writes=[("sS", sp)])

        def expb(tag):
            sp = SP[tag]
            _, sflat = sview(sS, sp, tag)
            _, eflat = sview(sE, sp, tag)
            tr.op("act", ACTV(eflat, sflat, Act.Exp, bias=ngc),
                  reads=[("sS", sp), rngc], writes=[("sE", sp)])

        def maskb():
            sp = SP["B"]
            tr.op("dve", TT(sE[:, sp, 0:1024], sE[:, sp, 0:1024], cmask[:], Alu.mult),
                  reads=[("sE", sp), ("cmask",)], writes=[("sE", sp)])

        def denom(tag):
            sp = SP[tag]
            e3, _ = sview(sE, sp, tag)
            if tag == "B":
                tr.op("dve", RED(dpart[:].rearrange("p (b q) -> p b q", b=4), e3,
                                 Alu.add),
                      reads=[("sE", sp)], writes=[("dpart",)])
            else:
                j = tag
                tr.op("dve", RED(rtmp[:, 16 * (j - 1):16 * j], e3, Alu.add),
                      reads=[("sE", sp)], writes=[("rtd", j)])

        def wview(sp, tag):
            w = WID[tag]
            flat = sW[:, sp, 0:w]
            if tag == "B":
                return flat.rearrange("p (b q k) -> p b q k", b=4, q=16)
            return flat.rearrange("p (q k) -> p q k", q=16)

        def vmult(tag):
            sp = SP[tag]
            e3, _ = sview(sE, sp, tag)
            _, _, vv = qkv_aps(tag)
            w3 = wview(sp, tag)
            tr.op("gp", TT(w3, e3, vv, Alu.mult),
                  reads=[("sE", sp), rv], writes=[("sW", sp)])

        def numer(tag):
            sp = SP[tag]
            w3 = wview(sp, tag)
            if tag == "B":
                tr.op("dve", RED(npart[:].rearrange("p (b q) -> p b q", b=4), w3,
                                 Alu.add),
                      reads=[("sW", sp)], writes=[("npart",)])
            else:
                j = tag
                tr.op("dve", RED(rtmp[:, 48 + 16 * (j - 1):48 + 16 * j], w3, Alu.add),
                      reads=[("sW", sp)], writes=[("rtn", j)])

        smult("B"); smult(1)
        expb("B")
        maskb(); denom("B")
        smult(2)
        vmult("B")
        expb(1)
        numer("B"); denom(1)
        smult(3)
        vmult(1)
        expb(2)
        numer(1); denom(2)
        vmult(2)
        expb(3)
        numer(2); denom(3)
        vmult(3)
        numer(3)
        tr.op("dve", TT(dpart[:, 16:64], dpart[:, 16:64], rtmp[:, 0:48], Alu.add),
              reads=[("rtd", 1), ("rtd", 2), ("rtd", 3), ("dpart",)],
              writes=[("dpart",)])
        tr.op("dve", TT(npart[:, 16:64], npart[:, 16:64], rtmp[:, 48:96], Alu.add),
              reads=[("rtn", 1), ("rtn", 2), ("rtn", 3), ("npart",)],
              writes=[("npart",)])
        tr.op("dve", RECIP(dpart[:], dpart[:]), reads=[("dpart",)], writes=[("dpart",)])
        tr.op("dve", TT(sA[:, h * C:(h + 1) * C], npart[:], dpart[:], Alu.mult),
              reads=[("npart",), ("dpart",)], writes=[("sA", h)])

    def emit_attn_post(mp, t):
        gt = mp * NT + t
        rall = [("sA", h) for h in range(NH)]
        if dbg:
            dbg_dump(dbg_d["a_dbg"].rearrange("(t p) d -> t p d", p=128)[gt],
                     sA[:], rall)
        for dc in range(DCH):
            bk = 2 + (dc % 2)
            pbb = pb[bk][:].bitcast(BF16)[:, 0:128]
            tr.group("pe", [TRP(pbb, sA[:, dc * 128:(dc + 1) * 128], identb[:])],
                     reads=rall + [("identb",)], writes=[PBANK(bk)])
            tr.op("act", ACTV(atstg[:, dc * 128:(dc + 1) * 128], pbb, Act.Copy),
                  reads=[PBANK(bk)], writes=[("atstg",)])
        tr.dma("sync", "at", at_d[gt], atstg[:], reads=[("atstg",)], writes=[("atD", gt)])

    def emit_w2_load(e, half=None):
        for h in ((0, 1) if half is None else (half,)):
            tr.dma("gp", f"w2c{h}", w2res[:, h * 8:(h + 1) * 8, :],
                   wmoe_d[e, h * 8:(h + 1) * 8, :, 1024:2048].rearrange(
                       "c p f -> p c f"),
                   writes=[("w2r", h)])

    def emit_w1_dma(e, q):
        sl = q % 2
        tr.dma("gp", "w1s%d" % sl, w1s[:, sl, :, :],
               wmoe_d[e, q * 4:(q + 1) * 4, :, 0:1024].rearrange("c p f -> p c f"),
               writes=[("w1s", sl)])

    def emit_moe_unit(mp, sweep, e, tb, drain=lambda k: None, prefetch=lambda: None, dq=(1, 1, 2, 2)):
        wsel = [("xnT", dc, tb * TSN + ts) for dc in range(DCH) for ts in range(TSN)]

        def hgroup(hch):
            sl = (hch // 4) % 2
            hb = hch % 2
            w1c = w1s[:, sl, hch % 4, :]
            mms = [MM(pb[hb][:][:, 0:TB], w1c[:, dc * 128:(dc + 1) * 128],
                      xnT[:, dc, tb * TB:(tb + 1) * TB], dc == 0, dc == DCH - 1)
                   for dc in range(DCH)]
            tr.group("pe", mms, reads=wsel + [("w1s", sl)], writes=[PBANK(hb)])
            tr.op("act", ACTV(hstg[:, hch, :], pb[hb][:][:, 0:TB], Act.Relu,
                              bias=b1s[:, e * 16 + hch:e * 16 + hch + 1]),
                  reads=[PBANK(hb), ("b1s",)], writes=[("hstg", hch)])

        for q in range(4):
            emit_w1_dma(e, q)
            if q >= 2:
                drain(dq[q - 2])
            for hq in range(4):
                hgroup(q * 4 + hq)
        drain(dq[2])
        for hch in range(HCH):
            first, last = hch == 0, hch == HCH - 1
            for ts in range(TSN):
                mms = []
                for fh in range(2):
                    ap_out = pb[2 * ts + fh][:][:, 0:512]
                    mms.append(MM(ap_out, hstg[:, hch, ts * 128:(ts + 1) * 128],
                                  w2res[:, hch, fh * 512:(fh + 1) * 512],
                                  first, last))
                tr.group("pe", mms, reads=[("hstg", hch), ("w2r", hch // 8)],
                         writes=[PBANK(2 * ts), PBANK(2 * ts + 1)])
        for ts in range(TSN):
            t = tb * TSN + ts
            for fh in range(2):
                bk = 2 * ts + fh
                dst = acc[:, t, fh * 512:(fh + 1) * 512]
                areg = ("acc", t, fh)
                if sweep == 0:
                    scol = oh1s[:, t, e:e + 1]
                    screg = ("oh1s", t)
                else:
                    scol = w2sc[:, t, e:e + 1]
                    screg = ("w2sc", t)
                if sweep == 0 and e == 0:
                    tr.op("dve", TS(dst, pb[bk][:][:, 0:512], scol, Alu.mult),
                          reads=[PBANK(bk), screg], writes=[areg])
                else:
                    tr.op("dve", STT(dst, pb[bk][:][:, 0:512], scol, dst,
                                     Alu.mult, Alu.add),
                          reads=[PBANK(bk), screg, areg], writes=[areg])
        prefetch()
        drain(dq[3])

    def emit_b2_unit(mp, sweep, tb, maskT, mkey):
        mtiles = [(mkey, tb * TSN + i) for i in range(TSN)]
        for ts in range(TSN):
            t = tb * TSN + ts
            tok0 = tb * TB + ts * 128
            for fh in range(2):
                bk = 2 * ts + fh
                ap_out = pb[bk][:][:, 0:512]
                mms = [MM(ap_out, maskT[:, tok0:tok0 + 128],
                          b2hi[:, fh * 512:(fh + 1) * 512], True, False),
                       MM(ap_out, maskT[:, tok0:tok0 + 128],
                          b2lo[:, fh * 512:(fh + 1) * 512], False, True)]
                tr.group("pe", mms, reads=mtiles + [("b2hi",), ("b2lo",)],
                         writes=[PBANK(bk)])
                dst = acc[:, t, fh * 512:(fh + 1) * 512]
                areg = ("acc", t, fh)
                if sweep == 0:
                    tr.op("dve", TT(dst, pb[bk][:][:, 0:512], dst, Alu.add),
                          reads=[PBANK(bk), areg], writes=[areg])
                else:
                    tr.op("dve", STT(dst, pb[bk][:][:, 0:512], v2s[:, t:t + 1],
                                     dst, Alu.mult, Alu.add),
                          reads=[PBANK(bk), ("v2s", t), areg], writes=[areg])

    def emit_p3_tile(mp, t):
        par = t % 2
        gt = mp * NT + t
        xreg = ("xls", par)
        tr.dma("sync", "xin0" if par == 0 else "xin1", xls[:, par, :], x_t[gt], writes=[xreg])
        xnreg = ("scA", par)
        tr.op("act", ACTV(scA[:, par, :], xls[:, par, :], Act.Identity,
                          bias=lnst[:, t, 0:1], scale=lnst[:, t, 1:2]),
              reads=[xreg, ("lnst", t)], writes=[xnreg])
        rall = [("acc", t, 0), ("acc", t, 1)]
        tr.op("dve", STT(scD[:, 1, :], acc[:, t, :], ALPHA, scA[:, par, :],
                         Alu.mult, Alu.add),
              reads=rall + [xnreg], writes=[("bst",)])
        tr.op("dve", STT(acc[:, t, :], acc[:, t, :], v1s[:, t:t + 1], scA[:, par, :],
                         Alu.mult, Alu.add),
              reads=rall + [xnreg, ("v1s", t)], writes=rall)
        for dc in range(DCH):
            bk = 6 + (dc % 2)
            tr.group("pe", [TRP(pb[bk][:][:, 0:128],
                                scD[:, 1, dc * 128:(dc + 1) * 128], identf[:])],
                     reads=[("bst",), ("identf",)], writes=[PBANK(bk)])
            tr.op("act", ACTV(xnT[:, dc, t * 128:(t + 1) * 128], pb[bk][:][:, 0:128],
                              Act.Copy),
                  reads=[PBANK(bk)], writes=[("xnT", dc, t)])
        tr.op("dve", TS(g8d[:], w2sc[:, t, :], 0.0, Alu.is_gt),
              reads=[("w2sc", t)], writes=[("g8d",)])
        tr.group("pe", [TRP(pb[4][:][0:E, 0:128], g8d[:], identf[:])],
                 reads=[("g8d",), ("identf",)], writes=[PBANK(4)])
        tr.op("act", ACTV(mT[:, t * 128:(t + 1) * 128], pb[4][:][0:E, 0:128],
                          Act.Copy),
              reads=[PBANK(4)], writes=[("mT", t)])

    def emit_epilogue_pre():
        tr.dma("gp", "w2c0", wos, wo_d.rearrange("(c p) f -> p c f", p=128),
               writes=[("wos",)])
        tr.dma("sync", "cst", scD[0:1, 0, :], gamma_d, writes=[("grow",), ("body",)])
        tr.dma("sync", "cst", scD[0:1, 1, :], beta_d, writes=[("brow",), ("bst",)])
        for src_key, dstap, dkey in ((("grow",), grep, "grep"), (("brow",), brep, "brep")):
            row = scD[0:1, 0, :] if dkey == "grep" else scD[0:1, 1, :]
            for half in range(2):
                tr.group("pe", [MM(pb[4 + half][:][:, 0:512], ones1f[:],
                                   row[:, half * 512:(half + 1) * 512], True, True)],
                         reads=[src_key, ("onesf",)], writes=[PBANK(4 + half)])
                tr.op("dve", CPY(dstap[:, half * 512:(half + 1) * 512],
                                 pb[4 + half][:][:, 0:512]),
                      reads=[PBANK(4 + half)], writes=[(dkey,)])

    def emit_epilogue_tile(mp, t):
        par = t % 2
        gt = mp * NT + t
        rall = [("acc", t, 0), ("acc", t, 1)]
        if dbg:
            dbg_dump(dbg_d["z_dbg"].rearrange("(t p) d -> t p d", p=128)[gt],
                     acc[:, t, :], rall)
        mu, rr = scS[:, 0:1], scS[:, 1:2]
        ln_stats(acc[:, t, :], rall, mu, rr)
        tr.op("act", ACTV(scA[:, 0, :], acc[:, t, :], Act.Identity, bias=mu, scale=rr),
              reads=rall + [("mu",), ("var",)], writes=[("ep0",)])
        tr.op("dve", TT(scA[:, 0, :], scA[:, 0, :], grep, Alu.mult),
              reads=[("ep0",), ("grep",)], writes=[("ep0",)])
        tr.op("dve", TT(scA[:, 0, :], scA[:, 0, :], brep, Alu.add),
              reads=[("ep0",), ("brep",)], writes=[("ep0",)])
        tr.op("act", ACTV(scA[:, 1, :], scA[:, 0, :], Act.Exp, scale=-1.0),
              reads=[("ep0",)], writes=[("ep1",)])
        tr.op("dve", TS(scA[:, 1, :], scA[:, 1, :], 1.0, Alu.add),
              reads=[("ep1",)], writes=[("ep1",)])
        tr.op("dve", RECIP(scA[:, 1, :], scA[:, 1, :]), reads=[("ep1",)],
              writes=[("ep1",)])
        tr.op("dve", TT(scA[:, 1, :], scA[:, 1, :], scA[:, 0, :], Alu.mult),
              reads=[("ep1",), ("ep0",)], writes=[("ep1",)])
        abuf = t % 2
        tr.dma("sync", "at", atld[:, abuf, :], at_d[gt], reads=[("atD", gt)], writes=[("atld", abuf)])
        mms = []
        for hc in range(DCH):
            for nh in range(2):
                mms.append(MM(pb[nh][:][:, 0:512],
                              atld[:, abuf, hc * 128:(hc + 1) * 128],
                              wos[:, hc, nh * 512:(nh + 1) * 512], hc == 0, False))
        for nh in range(2):
            mms.append(MM(pb[nh][:][:, 0:512], sel4[:, 384:512],
                          bqkvo_s[:, nh * 512:(nh + 1) * 512], False, True))
        tr.group("pe", mms, reads=[("atld", abuf), ("wos",), ("sel4",), ("bqkvo",)],
                 writes=[PBANK(0), PBANK(1)])
        xreg = ("xls", par)
        tr.dma("sync", "xin0" if par == 0 else "xin1", xls[:, par, :], x_t[gt], writes=[xreg])
        for nh in range(2):
            sl = slice(nh * 512, (nh + 1) * 512)
            tr.op("dve", TT(scD[:, 1, sl], pb[nh][:][:, 0:512], scA[:, 1, sl],
                            Alu.mult),
                  reads=[PBANK(nh), ("ep1",)], writes=[("bst",)])
        tr.op("dve", TT(scD[:, 1, :], scD[:, 1, :], xls[:, par, :], Alu.add),
              reads=[("bst",), xreg], writes=[("bst",)])
        tr.dma("sync", "out", out_t[gt], scD[:, 1, :], reads=[("bst",)])

    PH = int(os.environ.get("KR_PHASES", "9"))
    for mp in range(MP):
        for t in range(NT):
            emit_p0_tile(mp, t)
        tr.barrier()
        if PH < 2:
            continue

        qkv_done = [0]
        spc[0] = 0
        it_ptr = [0]
        iters = []
        for t in range(NT):
            for h in range(NH):
                iters.append((t, h, False))
            iters.append((t, None, True))

        def need_pair(t):
            while qkv_done[0] <= t:
                emit_qkv_pair(mp, qkv_done[0])
                qkv_done[0] += 2

        def drain_attn(k):
            for _ in range(k):
                if it_ptr[0] >= len(iters):
                    return
                t, h, post = iters[it_ptr[0]]
                need_pair(t)
                if post:
                    emit_attn_post(mp, t)
                else:
                    emit_attn_iter(mp, t, h)
                it_ptr[0] += 1

        def drain_rest():
            drain_attn(len(iters))

        for e in range(E):
            if e == 0:
                emit_w2_load(0)
            for tb in range(NTB):
                last = tb == NTB - 1
                pf = (lambda e=e: emit_w2_load(e + 1)) if (last and e < E - 1) \
                    else (lambda: None)
                emit_moe_unit(mp, 0, e, tb, drain=drain_attn, prefetch=pf)
        for tb in range(NTB):
            emit_b2_unit(mp, 0, tb, mT, "mT")
        drain_attn(max(0, (NT - 2) * (NH + 1) - it_ptr[0]))
        need_pair(NT - 1)
        if PH < 4:
            drain_rest()
            tr.barrier()
            continue
        if dbg:
            for t in range(NT):
                dbg_dump(dbg_d["eo1_dbg"].rearrange("(t p) d -> t p d", p=128)[mp * NT + t],
                         acc[:, t, :], [("acc", t, 0), ("acc", t, 1)])
        for t in range(NT):
            emit_p3_tile(mp, t)
            drain_attn(1)
        if PH < 5:
            drain_rest()
            tr.barrier()
            continue
        for e in range(E):
            if e == 0:
                emit_w2_load(0)
            for tb in range(NTB):
                last = tb == NTB - 1
                pf = (lambda e=e: emit_w2_load(e + 1)) if (last and e < E - 1) \
                    else (lambda: None)
                emit_moe_unit(mp, 1, e, tb, drain=drain_attn, prefetch=pf,
                              dq=(1, 1, 1, 0))
        for tb in range(NTB):
            emit_b2_unit(mp, 1, tb, mT, "mT")
        drain_rest()
        tr.barrier()
        if PH < 6:
            continue
        emit_epilogue_pre()
        for t in range(NT):
            emit_epilogue_tile(mp, t)
        tr.barrier()

    if tr.dma_cnt["out"] > 0:
        tr.stream["sync"].append(
            lambda eng, sem=tr.dma_sem["out"], v=tr.dma_cnt["out"]: eng.wait_ge(sem, v))

    with nc.Block() as block:
        @block.tensor
        def _(eng):
            for f in tr.stream["pe"]:
                f(eng)

        @block.vector
        def _(eng):
            for f in tr.stream["dve"]:
                f(eng)

        @block.scalar
        def _(eng):
            for f in tr.stream["act"]:
                f(eng)

        @block.gpsimd
        def _(eng):
            for f in tr.stream["gp"]:
                f(eng)

        @block.sync
        def _(eng):
            for f in tr.stream["sync"]:
                f(eng)

    return nc, es


def prep_inputs(inputs, TOK):
    f32 = lambda a: np.ascontiguousarray(np.asarray(a), dtype=np.float32)
    Wq, Wk, Wv = (f32(inputs["L0_attn_Wq"]), f32(inputs["L0_attn_Wk"]),
                  f32(inputs["L0_attn_Wv"]))
    wqkv = np.concatenate([Wq, Wk, Wv], axis=1)
    W1, W2 = f32(inputs["L0_ffn_W1"]), f32(inputs["L0_ffn_W2"])
    wmoe = np.empty((E, HCH, 128, 2048), dtype=np.float32)
    for e in range(E):
        for hc in range(HCH):
            w1p = W1[e][:, hc * 128:(hc + 1) * 128].reshape(DCH, 128, 128)
            wmoe[e, hc, :, 0:1024] = w1p.transpose(1, 0, 2).reshape(128, 1024)
            wmoe[e, hc, :, 1024:2048] = W2[e][hc * 128:(hc + 1) * 128, :]
    bqkvo = np.stack([f32(inputs["L0_attn_bq"]), f32(inputs["L0_attn_bk"]),
                      f32(inputs["L0_attn_bv"]), f32(inputs["L0_attn_bo"])], axis=0)
    b1 = f32(inputs["L0_ffn_b1"])
    b1p = b1.reshape(E, HCH, 128).transpose(2, 0, 1).reshape(128, E * HCH)
    cm = np.zeros((4, 16, 16), dtype=np.float32)
    for q in range(16):
        cm[:, q, :q + 1] = 1.0
    cmask = np.broadcast_to(cm.reshape(1, 1024), (128, 1024)).astype(ml_dtypes.bfloat16)
    shared = {
        "wqkv": wqkv,
        "gw": f32(inputs["L0_ffn_gate_W"]),
        "gb": f32(inputs["L0_ffn_gate_b"]).reshape(E, 1),
        "wmoe": wmoe,
        "wo": f32(inputs["L0_attn_Wo"]),
        "bqkvo": bqkvo,
        "b1p": np.ascontiguousarray(b1p),
        "b2": f32(inputs["L0_ffn_b2"]),
        "gamma": f32(inputs["L0_ffn_ln_gamma"]).reshape(1, D),
        "beta": f32(inputs["L0_ffn_ln_beta"]).reshape(1, D),
        "identf": np.eye(128, dtype=np.float32),
        "identb": np.eye(128, dtype=ml_dtypes.bfloat16),
        "ones1b": np.ones((1, 128), dtype=ml_dtypes.bfloat16),
        "ones1f": np.ones((1, 128), dtype=np.float32),
        "cmask": np.ascontiguousarray(cmask),
        "fconst": np.tile(np.asarray([FCONST_VALS], dtype=np.float32), (128, 1)),
        "sel4": np.ascontiguousarray(
            np.repeat(np.eye(4, dtype=ml_dtypes.bfloat16), 128, axis=1)),
    }
    x = f32(inputs["L0_attn_x"]).reshape(-1, D)
    xs = [np.ascontiguousarray(x[i * TOK:(i + 1) * TOK]) for i in range(N_CORES)]
    return shared, xs


def kernel(**inputs):
    x = np.asarray(inputs["L0_attn_x"])
    n_tok = x.shape[0] * x.shape[1]
    TOK = n_tok // N_CORES
    shared, xs = prep_inputs(inputs, TOK)
    nc, es = build_program(TOK, dbg=False)
    try:
        in_maps = [dict(shared, x=xs[i]) for i in range(N_CORES)]
        res = run_bass_kernel_spmd(nc, in_maps, core_ids=list(range(N_CORES)))
    finally:
        es.close()
    out = np.concatenate([res.results[i]["out"] for i in range(N_CORES)], axis=0)
    return out.reshape(x.shape).astype(np.float32)


# version 17
# speedup vs baseline: 1.0127x; 1.0127x over previous
import os
import numpy as np
import ml_dtypes
from contextlib import ExitStack

import concourse.bass as bass
import concourse.mybir as mybir
from concourse.bass_utils import run_bass_kernel_spmd

F32 = mybir.dt.float32
BF16 = mybir.dt.bfloat16
Alu = mybir.AluOpType
Act = mybir.ActivationFunctionType
AX = mybir.AxisListType

N_CORES = 8
D = 1024
NH = 16
C = 64
E = 8
HF = 2048
DCH = D // 128
HCH = HF // 128
LN_EPS = 1e-5
ALPHA = 0.5
TB = 512
TSN = TB // 128
FCONST_VALS = (1e-5, 1.0 / 1024, -0.5, 0.125, -1.0, 1.0, 0.0, 2.0)


def TT(out, a, b, op):
    return lambda e, out=out, a=a, b=b, op=op: e.tensor_tensor(out, a, b, op)

def STT(out, in0, s, in1, op0, op1):
    return lambda e, out=out, in0=in0, s=s, in1=in1, op0=op0, op1=op1: \
        e.scalar_tensor_tensor(out, in0, s, in1, op0, op1)

def TS(out, a, s, op):
    return lambda e, out=out, a=a, s=s, op=op: e.tensor_scalar(out, a, s, None, op)

def CPY(out, a):
    return lambda e, out=out, a=a: e.tensor_copy(out, a)

def ACTV(out, a, f, bias=0.0, scale=1.0):
    return lambda e, out=out, a=a, f=f, bias=bias, scale=scale: \
        e.activation(out, a, f, bias=bias, scale=scale)

def RED(out, a, op, neg=False, absv=False):
    return lambda e, out=out, a=a, op=op, neg=neg, absv=absv: \
        e.tensor_reduce(out, a, AX.X, op, negate=neg, apply_absolute_value=absv)

def AMR(out, accum, in0, in1, scale, bias):
    return lambda e, out=out, accum=accum, in0=in0, in1=in1, scale=scale, bias=bias: \
        e.affine_mul_reduce(out=out, accum_out=accum, in0=in0, in1=in1,
                            scale=scale, bias=bias)

def RECIP(out, a):
    return lambda e, out=out, a=a: e.reciprocal(out, a)

def MM(out, l, r, st, sp):
    return lambda e, out=out, l=l, r=r, st=st, sp=sp: \
        e.matmul(out, l, r, start=st, stop=sp)

def TRP(out, a, ident):
    return lambda e, out=out, a=a, ident=ident: e.transpose(out, a, ident)


class Tracker:
    CE = ("pe", "dve", "act", "gp")

    def __init__(self, nc, es):
        self.nc = nc
        self.stream = {e: [] for e in ("pe", "dve", "act", "gp", "sync")}
        self.sem = {e: es.enter_context(nc.semaphore(f"sem_{e}")) for e in self.CE}
        self.cnt = {e: 0 for e in self.CE}
        self.dma_sem = {}
        self.dma_cnt = {}
        self.last_w = {}
        self.readers = {}
        self.hwm = {}

    def new_dma_sem(self, es, name):
        self.dma_sem[name] = es.enter_context(self.nc.semaphore("dq_" + name))
        self.dma_cnt[name] = 0

    def _wait(self, we, src):
        kind, name, cnt = src
        if kind == "d":
            cnt = self.dma_cnt[name]
        key = (we, kind, name)
        if self.hwm.get(key, -1) >= cnt:
            return
        self.hwm[key] = cnt
        sem = self.sem[name] if kind == "e" else self.dma_sem[name]
        self.stream[we].append(lambda eng, sem=sem, cnt=cnt: eng.wait_ge(sem, cnt))

    def _pre(self, we, reads, writes):
        for r in reads:
            lw = self.last_w.get(r)
            if lw is not None:
                self._wait(we, lw)
            if we in ("act", "dve") and isinstance(r, tuple) and r and r[0] == "pb":
                other = "dve" if we == "act" else "act"
                for rd in self.readers.get(r, ()):
                    if rd[0] == "e" and rd[1] == other:
                        self._wait(we, rd)
        for w in writes:
            lw = self.last_w.get(w)
            if lw is not None:
                self._wait(we, lw)
            for rd in self.readers.get(w, ()):
                self._wait(we, rd)

    def _post(self, kind, name, cnt, reads, writes):
        src = (kind, name, cnt)
        for r in reads:
            self.readers.setdefault(r, []).append(src)
        for w in writes:
            self.last_w[w] = src
            self.readers[w] = []

    def op(self, we, emit, reads=(), writes=()):
        self.group(we, [emit], reads, writes)

    def group(self, we, emits, reads=(), writes=()):
        self._pre(we, reads, writes)
        self.cnt[we] += 1
        cnt = self.cnt[we]
        sem = self.sem[we]

        def run(eng, emits=tuple(emits), sem=sem):
            n = len(emits)
            for i, em in enumerate(emits):
                inst = em(eng)
                if i == n - 1:
                    inst.then_inc(sem, 1)

        self.stream[we].append(run)
        self._post("e", we, cnt, reads, writes)

    def dma(self, qe, sem_name, out, in_, reads=(), writes=()):
        self._pre(qe, reads, writes)
        self.dma_cnt[sem_name] += 16
        cnt = self.dma_cnt[sem_name]
        sem = self.dma_sem[sem_name]

        def run(eng, out=out, in_=in_, sem=sem):
            eng.dma_start(out, in_).then_inc(sem, 16)

        self.stream[qe].append(run)
        self._post("d", sem_name, cnt, reads, writes)

    def barrier(self):
        snap = dict(self.cnt)
        dsnap = dict(self.dma_cnt)
        for we in self.CE + ("sync",):
            for se in self.CE:
                if se != we and snap[se] > 0:
                    self._wait(we, ("e", se, snap[se]))
            for dn, c in dsnap.items():
                if c > 0:
                    self._wait(we, ("d", dn, c))


def build_program(TOK, dbg=False):
    MP = 2 if TOK >= 2048 else 1
    MT = TOK // MP
    NT = MT // 128
    NTB = MT // TB

    nc = bass.Bass("TRN2", target_bir_lowering=False)
    es = ExitStack()

    def din(name, shape, dt=F32):
        return nc.dram_tensor(name, list(shape), dt, kind="ExternalInput").ap()

    x_d = din("x", (TOK, D))
    wqkv_d = din("wqkv", (D, 3 * D))
    gw_d = din("gw", (D, E))
    gb_d = din("gb", (E, 1))
    wmoe_d = din("wmoe", (E, HCH, 128, 2048))
    wo_d = din("wo", (D, D))
    bqkvo_d = din("bqkvo", (4, D))
    b1p_d = din("b1p", (128, 128))
    b2_d = din("b2", (E, D))
    gamma_d = din("gamma", (1, D))
    beta_d = din("beta", (1, D))
    identf_d = din("identf", (128, 128))
    identb_d = din("identb", (128, 128), BF16)
    ones1b_d = din("ones1b", (1, 128), BF16)
    ones1f_d = din("ones1f", (1, 128))
    cmask_d = din("cmask", (128, 1024), BF16)
    sel4_d = din("sel4", (4, 512), BF16)
    fconst_d = din("fconst", (128, 8))

    out_d = nc.dram_tensor("out", [TOK, D], F32, kind="ExternalOutput").ap()
    at_d = nc.dram_tensor("atspill", [MP * NT, 128, D], BF16, kind="Internal").ap()

    dbg_d = {}
    if dbg:
        for nm, sh in (("xn_dbg", (TOK, D)), ("gate_dbg", (TOK, E)),
                       ("sel_dbg", (TOK, 4)), ("a_dbg", (TOK, D)),
                       ("eo1_dbg", (TOK, D)), ("z_dbg", (TOK, D)),
                       ("qkv_dbg", (3, 128, D)),):
            dbg_d[nm] = nc.dram_tensor(nm, list(sh), F32, kind="ExternalOutput").ap()

    sb = lambda n, s, d=F32: es.enter_context(nc.sbuf_tensor(n, list(s), d))

    xnT = sb("xnT", (128, DCH, MT), BF16)
    acc = sb("acc", (128, NT, D), F32)
    mT = sb("mT", (E, MT), BF16)
    v1s = sb("v1s", (128, NT), F32)
    v2s = sb("v2s", (128, NT), F32)
    oh1s = sb("oh1s", (128, NT, E), F32)
    w2sc = sb("w2sc", (128, NT, E), F32)
    b1s = sb("b1s", (128, 128), F32)
    b2hi = sb("b2hi", (E, D), BF16)
    b2lo = sb("b2lo", (E, D), BF16)
    gbs = sb("gbs", (E, 1), F32)
    bqkvo_s = sb("bqkvo_s", (4, D), BF16)
    sel4 = sb("sel4_s", (4, 512), BF16)
    identf = sb("identf_s", (128, 128), F32)
    identb = sb("identb_s", (128, 128), BF16)
    ones1b = sb("ones1b_s", (1, 128), BF16)
    ones1f = sb("ones1f_s", (1, 128), F32)
    cmask = sb("cmask_s", (128, 1024), BF16)
    gws = sb("gws", (128, DCH * E), F32)
    fconst = sb("fconst_s", (128, 8), F32)
    sW = sb("sW", (128, 3, 1024), BF16)
    qkvw = sb("qkvw", (128, 2, 2, 1024), BF16)
    qt = sb("qt", (128, 2, D), BF16)
    kt = sb("kt", (128, 2, D), BF16)
    vt = sb("vt", (128, 2, D), BF16)
    sS = sb("sS", (128, 3, 1024), F32)
    sE = sb("sE", (128, 3, 1024), F32)
    sA = sb("sA", (128, D), BF16)
    atstg = sb("atstg", (128, D), BF16)
    w1s = sb("w1s", (128, 2, 4, 1024), BF16)
    w2res = sb("w2res", (128, HCH, 1024), BF16)
    hstg = sb("hstg", (128, HCH, TB), BF16)
    xls = sb("xls", (128, 2, D), F32)
    scA = sb("scA", (128, 2, D), F32)
    scD = sb("scD", (128, 2, D), F32)
    scS = sb("scS", (128, 8), F32)
    atst = sb("atst", (128, 2, 32), F32)
    atst2 = sb("atst2", (128, 2, 16), F32)
    lnst = sb("lnst", (128, NT, 2), F32)
    dpart = sb("dpart", (128, C), F32)
    npart = sb("npart", (128, C), F32)
    rtmp = sb("rtmp", (128, 96), F32)
    g8a = sb("g8a", (E, 128), F32)
    g8b = sb("g8b", (128, E), F32)
    g8c = sb("g8c", (128, E), F32)
    g8d = sb("g8d", (128, E), F32)
    mrow = sb("mrow", (128, 2, E), F32)
    dbgbuf = sb("dbgbuf", (128, D), F32) if dbg else None
    wos = w2res[:].rearrange("p a b -> p (a b)")[:, 0:DCH * D].rearrange(
        "p (c f) -> p c f", c=DCH)
    atld = hstg[:].rearrange("p a b -> p (a b)")[:, 0:2 * D].rearrange(
        "p (c f) -> p c f", c=2)
    grep = sS[:].rearrange("p a b -> p (a b)")[:, 0:D]
    brep = sS[:].rearrange("p a b -> p (a b)")[:, D:2 * D]
    b2f = scA[0:E, 0, :]

    pb = [es.enter_context(nc.psum_tensor(f"pb{i}", [128, 512], F32)) for i in range(8)]
    PBANK = lambda i: ("pb", i)

    tr = Tracker(nc, es)
    for s in (["xin0", "xin1", "wq0", "wq1", "cst", "cstg",
               "out", "at", "dbg"] + [f"w1s{i}" for i in range(4)]
              + [f"w2c{i}" for i in range(2)]):
        tr.new_dma_sem(es, s)

    x_t = x_d.rearrange("(t p) d -> t p d", p=128)
    out_t = out_d.rearrange("(t p) d -> t p d", p=128)

    def dbg_dump(dst, src_ap, reads):
        w = src_ap.shape[-1]
        tr.op("act", ACTV(dbgbuf[:, 0:w], src_ap, Act.Copy),
              reads=list(reads), writes=[("dbgB",)])
        tr.dma("sync", "dbg", dst, dbgbuf[:, 0:w], reads=[("dbgB",)])

    for dst, src, key in ((identf[:], identf_d, "identf"), (identb[:], identb_d, "identb"),
                          (ones1b[:], ones1b_d, "ones"), (ones1f[:], ones1f_d, "onesf"),
                          (b1s[:], b1p_d, "b1s"), (gbs[:], gb_d, "gbs")):
        tr.dma("sync", "cst", dst, src, writes=[(key,)])
    tr.dma("sync", "cst", gws[:].rearrange("p (c e) -> p c e", c=DCH),
           gw_d.rearrange("(c p) e -> p c e", p=128), writes=[("gws",)])
    tr.dma("gp", "cstg", cmask[:], cmask_d, writes=[("cmask",)])
    tr.dma("sync", "cst", fconst[:], fconst_d, writes=[("fconst",)])
    for _i, _v in enumerate(FCONST_VALS):
        nc.const_aps.register(F32, _v, fconst[:, _i:_i + 1])
    tr.dma("gp", "cstg", bqkvo_s[:], bqkvo_d, writes=[("bqkvo",)])
    tr.dma("sync", "cst", sel4[:], sel4_d, writes=[("sel4",)])
    tr.dma("sync", "cst", b2f, b2_d, writes=[("b2f",)])
    tr.op("act", ACTV(b2hi[:], b2f, Act.Copy), reads=[("b2f",)], writes=[("b2hi",)])
    tr.op("dve", TT(b2lo[:], b2f, b2hi[:], Alu.subtract),
          reads=[("b2f",), ("b2hi",)], writes=[("b2lo",)])
    tr.barrier()

    def ln_stats(xap, xregs, mu, rr):
        tr.op("dve", RED(mu, xap, Alu.add), reads=xregs, writes=[("mu",)])
        tr.op("dve", TS(mu, mu, -1.0 / D, Alu.mult), reads=[("mu",)], writes=[("mu",)])
        tr.op("dve", lambda e, xap=xap, mu=mu, rr=rr: e.scalar_tensor_tensor(
            scD[:, 0, :], xap, mu, xap, Alu.add, Alu.mult, accum_out=rr),
              reads=xregs + [("mu",)], writes=[("var",), ("body",)])
        tr.op("act", ACTV(rr, rr, Act.Ln, bias=LN_EPS, scale=1.0 / D),
              reads=[("var",)], writes=[("var",)])
        tr.op("act", ACTV(rr, rr, Act.Exp, scale=-0.5), reads=[("var",)], writes=[("var",)])
        tr.op("dve", TT(mu, mu, rr, Alu.mult), reads=[("mu",), ("var",)], writes=[("mu",)])

    P0LITE = os.environ.get("KR_P0LITE", "")
    def emit_p0_tile(mp, t):
        par = t % 2
        gt = mp * NT + t
        xreg = ("xls", par)
        tr.dma("sync", "xin0" if par == 0 else "xin1", xls[:, par, :], x_t[gt], writes=[xreg])
        mu, rr = lnst[:, t, 0:1], lnst[:, t, 1:2]
        ln_stats(xls[:, par, :], [xreg], mu, rr)
        tr.op("dve", CPY(scS[:, 0:2], lnst[:, t, :]),
              reads=[("mu",), ("var",)], writes=[("lnst", t)])
        xnreg = ("scA", par)
        tr.op("act", ACTV(scA[:, par, :], xls[:, par, :], Act.Identity, bias=mu, scale=rr),
              reads=[xreg, ("lnst", t)], writes=[xnreg])
        if dbg:
            dbg_dump(dbg_d["xn_dbg"].rearrange("(t p) d -> t p d", p=128)[gt],
                     scA[:, par, :], [xnreg])
        if P0LITE == "nope":
            return
        for dc in range(DCH):
            bk = 6 + (dc % 2)
            tr.group("pe", [TRP(pb[bk][:][:, 0:128],
                                scA[:, par, dc * 128:(dc + 1) * 128], identf[:])],
                     reads=[xnreg, ("identf",)], writes=[PBANK(bk)])
            tr.op("act", ACTV(xnT[:, dc, t * 128:(t + 1) * 128], pb[bk][:][:, 0:128],
                              Act.Copy),
                  reads=[PBANK(bk)], writes=[("xnT", dc, t)])
            xf = scD[:, 1, (dc % 2) * 128:(dc % 2) * 128 + 128]
            tr.op("act", ACTV(xf, pb[bk][:][:, 0:128], Act.Copy),
                  reads=[PBANK(bk)], writes=[("xf32", dc % 2)])
            tr.group("pe", [MM(pb[5][:][0:E, 0:128], gws[:, dc * E:(dc + 1) * E],
                               xf, dc == 0, dc == DCH - 1)],
                     reads=[("xf32", dc % 2), ("gws",)], writes=[PBANK(5)])
        if P0LITE in ("nosmall", "nogate"):
            return
        tr.op("act", ACTV(g8a[:], pb[5][:][0:E, 0:128], Act.Identity, bias=gbs[:]),
              reads=[PBANK(5), ("gbs",)], writes=[("g8a",)])
        tr.group("pe", [TRP(pb[4][:][:, 0:E], g8a[:], identf[0:E, 0:E])],
                 reads=[("g8a",), ("identf",)], writes=[PBANK(4)])
        tr.op("dve", CPY(g8b[:], pb[4][:][:, 0:E]), reads=[PBANK(4)], writes=[("g8b",)])
        if dbg:
            dbg_dump(dbg_d["gate_dbg"].rearrange("(t p) e -> t p e", p=128)[gt],
                     g8b[:], [("g8b",)])
        mg, sg = scS[:, 2:3], scS[:, 3:4]
        tr.op("dve", RED(mg, g8b[:], Alu.max, neg=True), reads=[("g8b",)], writes=[("mg",)])
        tr.op("act", ACTV(g8c[:], g8b[:], Act.Exp, bias=mg),
              reads=[("g8b",), ("mg",)], writes=[("g8c",)])
        tr.op("dve", RED(sg, g8c[:], Alu.add), reads=[("g8c",)], writes=[("sg",)])
        tr.op("dve", RECIP(sg, sg), reads=[("sg",)], writes=[("sg",)])
        tr.op("dve", TS(g8c[:], g8c[:], sg, Alu.mult), reads=[("g8c",), ("sg",)],
              writes=[("g8c",)])
        va, vb = v1s[:, t:t + 1], v2s[:, t:t + 1]
        tr.op("dve", RED(va, g8c[:], Alu.max), reads=[("g8c",)], writes=[("v1s", t)])
        tr.op("dve", TS(mrow[:, 0, :], g8c[:], va, Alu.is_equal),
              reads=[("g8c",), ("v1s", t)], writes=[("mrow", 0)])
        tr.op("dve", STT(g8d[:], mrow[:, 0, :], -1e30, g8c[:], Alu.mult, Alu.add),
              reads=[("mrow", 0), ("g8c",)], writes=[("g8d",)])
        tr.op("dve", RED(vb, g8d[:], Alu.max), reads=[("g8d",)], writes=[("v2s", t)])
        tr.op("dve", TS(mrow[:, 1, :], g8d[:], vb, Alu.is_equal),
              reads=[("g8d",), ("v2s", t)], writes=[("mrow", 1)])
        tr.op("dve", CPY(oh1s[:, t, :], mrow[:, 0, :]),
              reads=[("mrow", 0)], writes=[("oh1s", t)])
        tr.op("dve", TS(w2sc[:, t, :], mrow[:, 1, :], vb, Alu.mult),
              reads=[("mrow", 1), ("v2s", t)], writes=[("w2sc", t)])
        tr.group("pe", [TRP(pb[4][:][0:E, 0:128], mrow[:, 0, :], identf[:])],
                 reads=[("mrow", 0), ("identf",)], writes=[PBANK(4)])
        tr.op("act", ACTV(mT[:, t * 128:(t + 1) * 128], pb[4][:][0:E, 0:128],
                          Act.Copy),
              reads=[PBANK(4)], writes=[("mT", t)])
        if dbg:
            tr.op("dve", CPY(rtmp[:, 0:1], va), reads=[("v1s", t)], writes=[("seldmp",)])
            tr.op("dve", CPY(rtmp[:, 1:2], vb), reads=[("v2s", t)], writes=[("seldmp",)])
            tr.op("dve", RED(rtmp[:, 2:3], g8c[:], Alu.add), reads=[("g8c",)],
                  writes=[("seldmp",)])
            tr.op("dve", RED(rtmp[:, 3:4], g8d[:], Alu.max), reads=[("g8d",)],
                  writes=[("seldmp",)])
            dbg_dump(dbg_d["sel_dbg"].rearrange("(t p) e -> t p e", p=128)[gt],
                     rtmp[:, 0:4], [("seldmp",)])

    def emit_qkv_pair(mp, t0):
        tiles = [t0, t0 + 1] if t0 + 1 < NT else [t0]
        dsts = ((qt, "qt"), (kt, "kt"), (vt, "vt"))
        for proj in range(3):
            for qtr in range(4):
                sl = qtr % 2
                tr.dma("gp", f"wq{sl}", qkvw[:, sl, :, :],
                       wqkv_d[qtr * 256:(qtr + 1) * 256,
                              proj * D:(proj + 1) * D].rearrange(
                           "(c p) f -> p c f", p=128),
                       writes=[("qkvw", sl)])
                for ti, t in enumerate(tiles):
                    mms = []
                    for dq in range(2):
                        dc = qtr * 2 + dq
                        for nh in range(2):
                            wv = qkvw[:, sl, dq, nh * 512:(nh + 1) * 512]
                            mms.append(MM(pb[2 + 2 * ti + nh][:][:, 0:512],
                                          xnT[:, dc, t * 128:(t + 1) * 128], wv,
                                          dc == 0, False))
                    if qtr == 3:
                        for nh in range(2):
                            mms.append(MM(pb[2 + 2 * ti + nh][:][:, 0:512],
                                          sel4[:, proj * 128:(proj + 1) * 128],
                                          bqkvo_s[:, nh * 512:(nh + 1) * 512],
                                          False, True))
                    rds = [("xnT", qtr * 2 + dq, t) for dq in range(2)] + \
                          [("qkvw", sl), ("sel4",), ("bqkvo",)]
                    tr.group("pe", mms, reads=rds,
                             writes=[PBANK(2 + 2 * ti), PBANK(3 + 2 * ti)])
            for ti, t in enumerate(tiles):
                par = t % 2
                dst, key = dsts[proj]
                scl = 0.125 if proj == 0 else 1.0
                for nh in range(2):
                    tr.op("act", ACTV(dst[:, par, nh * 512:(nh + 1) * 512],
                                      pb[2 + 2 * ti + nh][:][:, 0:512], Act.Copy,
                                      scale=scl),
                          reads=[PBANK(2 + 2 * ti + nh)], writes=[(key, par)])
                if dbg and t == 0:
                    dbg_dump(dbg_d["qkv_dbg"][proj], dst[:, par, :], [(key, par)])
        for t in tiles:
            par = t % 2
            qv3 = qt[:, par, :].rearrange("p (h c) -> p h c", h=NH)
            kv3 = kt[:, par, :].rearrange("p (h c) -> p h c", h=NH)
            tr.op("dve", RED(atst[:, par, 0:16], qv3, Alu.max, absv=True),
                  reads=[("qt", par)], writes=[("qmax", par)])
            tr.op("dve", RED(atst[:, par, 16:32], kv3, Alu.max, absv=True, neg=True),
                  reads=[("kt", par)], writes=[("kmax", par)])
            tr.op("dve", TT(atst2[:, par, :], atst[:, par, 0:16], atst[:, par, 16:32],
                            Alu.mult),
                  reads=[("qmax", par), ("kmax", par)], writes=[("ngcs", par)])

    spc = [0]
    def emit_attn_iter(mp, t, h):
        par = t % 2
        qs = qt[:, par, h * C:(h + 1) * C]
        ks = kt[:, par, h * C:(h + 1) * C]
        vs = vt[:, par, h * C:(h + 1) * C]
        rq, rk, rv = ("qt", par), ("kt", par), ("vt", par)
        ngc = atst2[:, par, h:h + 1]
        rngc = ("ngcs", par)

        WID = {"B": 1024, 1: 256, 2: 512, 3: 768}
        base = spc[0]
        SP = {"B": base % 3, 1: (base + 1) % 3, 2: (base + 2) % 3, 3: (base + 3) % 3}
        spc[0] = base + 4

        def qkv_aps(tag):
            if tag == "B":
                qv = qs.rearrange("p (b q) -> p b q", b=4).unsqueeze(3) \
                    .broadcast_to([128, 4, 16, 16])
                kv = ks.rearrange("p (b k) -> p b k", b=4).unsqueeze(2) \
                    .broadcast_to([128, 4, 16, 16])
                vv = vs.rearrange("p (b k) -> p b k", b=4).unsqueeze(2) \
                    .broadcast_to([128, 4, 16, 16])
            else:
                j = tag
                w = 16 * j
                qv = qs[:, 16 * j:16 * j + 16].unsqueeze(2).broadcast_to([128, 16, w])
                kv = ks[:, 0:w].unsqueeze(1).broadcast_to([128, 16, w])
                vv = vs[:, 0:w].unsqueeze(1).broadcast_to([128, 16, w])
            return qv, kv, vv

        def sview(buf, sp, tag):
            w = WID[tag]
            flat = buf[:, sp, 0:w]
            if tag == "B":
                return flat.rearrange("p (b q k) -> p b q k", b=4, q=16), flat
            return flat.rearrange("p (q k) -> p q k", q=16), flat

        def smult(tag):
            sp = SP[tag]
            qv, kv, _ = qkv_aps(tag)
            o3, _ = sview(sS, sp, tag)
            tr.op("gp", TT(o3, qv, kv, Alu.mult), reads=[rq, rk],
                  writes=[("sS", sp)])

        def expb(tag):
            sp = SP[tag]
            _, sflat = sview(sS, sp, tag)
            _, eflat = sview(sE, sp, tag)
            tr.op("act", ACTV(eflat, sflat, Act.Exp, bias=ngc),
                  reads=[("sS", sp), rngc], writes=[("sE", sp)])

        def maskb():
            sp = SP["B"]
            tr.op("dve", TT(sE[:, sp, 0:1024], sE[:, sp, 0:1024], cmask[:], Alu.mult),
                  reads=[("sE", sp), ("cmask",)], writes=[("sE", sp)])

        def denom(tag):
            sp = SP[tag]
            e3, _ = sview(sE, sp, tag)
            if tag == "B":
                tr.op("dve", RED(dpart[:].rearrange("p (b q) -> p b q", b=4), e3,
                                 Alu.add),
                      reads=[("sE", sp)], writes=[("dpart",)])
            else:
                j = tag
                tr.op("dve", RED(rtmp[:, 16 * (j - 1):16 * j], e3, Alu.add),
                      reads=[("sE", sp)], writes=[("rtd", j)])

        def wview(sp, tag):
            w = WID[tag]
            flat = sW[:, sp, 0:w]
            if tag == "B":
                return flat.rearrange("p (b q k) -> p b q k", b=4, q=16)
            return flat.rearrange("p (q k) -> p q k", q=16)

        def vmult(tag):
            sp = SP[tag]
            e3, _ = sview(sE, sp, tag)
            _, _, vv = qkv_aps(tag)
            w3 = wview(sp, tag)
            tr.op("gp", TT(w3, e3, vv, Alu.mult),
                  reads=[("sE", sp), rv], writes=[("sW", sp)])

        def numer(tag):
            sp = SP[tag]
            w3 = wview(sp, tag)
            if tag == "B":
                tr.op("dve", RED(npart[:].rearrange("p (b q) -> p b q", b=4), w3,
                                 Alu.add),
                      reads=[("sW", sp)], writes=[("npart",)])
            else:
                j = tag
                tr.op("dve", RED(rtmp[:, 48 + 16 * (j - 1):48 + 16 * j], w3, Alu.add),
                      reads=[("sW", sp)], writes=[("rtn", j)])

        smult("B"); smult(1)
        expb("B")
        maskb(); denom("B")
        smult(2)
        vmult("B")
        expb(1)
        numer("B"); denom(1)
        smult(3)
        vmult(1)
        expb(2)
        numer(1); denom(2)
        vmult(2)
        expb(3)
        numer(2); denom(3)
        vmult(3)
        numer(3)
        tr.op("dve", TT(dpart[:, 16:64], dpart[:, 16:64], rtmp[:, 0:48], Alu.add),
              reads=[("rtd", 1), ("rtd", 2), ("rtd", 3), ("dpart",)],
              writes=[("dpart",)])
        tr.op("dve", TT(npart[:, 16:64], npart[:, 16:64], rtmp[:, 48:96], Alu.add),
              reads=[("rtn", 1), ("rtn", 2), ("rtn", 3), ("npart",)],
              writes=[("npart",)])
        tr.op("dve", RECIP(dpart[:], dpart[:]), reads=[("dpart",)], writes=[("dpart",)])
        tr.op("dve", TT(sA[:, h * C:(h + 1) * C], npart[:], dpart[:], Alu.mult),
              reads=[("npart",), ("dpart",)], writes=[("sA", h)])

    def emit_attn_post(mp, t):
        gt = mp * NT + t
        rall = [("sA", h) for h in range(NH)]
        if dbg:
            dbg_dump(dbg_d["a_dbg"].rearrange("(t p) d -> t p d", p=128)[gt],
                     sA[:], rall)
        for dc in range(DCH):
            bk = 2 + (dc % 2)
            pbb = pb[bk][:].bitcast(BF16)[:, 0:128]
            tr.group("pe", [TRP(pbb, sA[:, dc * 128:(dc + 1) * 128], identb[:])],
                     reads=rall + [("identb",)], writes=[PBANK(bk)])
            tr.op("act", ACTV(atstg[:, dc * 128:(dc + 1) * 128], pbb, Act.Copy),
                  reads=[PBANK(bk)], writes=[("atstg",)])
        tr.dma("sync", "at", at_d[gt], atstg[:], reads=[("atstg",)], writes=[("atD", gt)])

    def emit_w2_load(e, half=None):
        for h in ((0, 1) if half is None else (half,)):
            tr.dma("gp", f"w2c{h}", w2res[:, h * 8:(h + 1) * 8, :],
                   wmoe_d[e, h * 8:(h + 1) * 8, :, 1024:2048].rearrange(
                       "c p f -> p c f"),
                   writes=[("w2r", h)])

    def emit_w1_dma(e, q):
        sl = q % 2
        tr.dma("gp", "w1s%d" % sl, w1s[:, sl, :, :],
               wmoe_d[e, q * 4:(q + 1) * 4, :, 0:1024].rearrange("c p f -> p c f"),
               writes=[("w1s", sl)])

    def emit_moe_unit(mp, sweep, e, tb, drain=lambda k: None, prefetch=lambda: None,
                      dq=(1, 1, 2, 2)):
        wsel = [("xnT", dc, tb * TSN + ts) for dc in range(DCH) for ts in range(TSN)]

        def hgroup(hch):
            sl = (hch // 4) % 2
            hb = hch % 2
            w1c = w1s[:, sl, hch % 4, :]
            mms = [MM(pb[hb][:][:, 0:TB], w1c[:, dc * 128:(dc + 1) * 128],
                      xnT[:, dc, tb * TB:(tb + 1) * TB], dc == 0, dc == DCH - 1)
                   for dc in range(DCH)]
            tr.group("pe", mms, reads=wsel + [("w1s", sl)], writes=[PBANK(hb)])
            tr.op("act", ACTV(hstg[:, hch, :], pb[hb][:][:, 0:TB], Act.Relu,
                              bias=b1s[:, e * 16 + hch:e * 16 + hch + 1]),
                  reads=[PBANK(hb), ("b1s",)], writes=[("hstg", hch)])

        for hch in range(8):
            hgroup(hch)
        drain(1)
        emit_w1_dma(e, 2)
        emit_w1_dma(e, 3)
        drain(1)
        for hch in range(8, HCH):
            hgroup(hch)
        drain(dq[2])
        for hch in range(HCH):
            first, last = hch == 0, hch == HCH - 1
            for ts in range(TSN):
                mms = []
                for fh in range(2):
                    ap_out = pb[2 * ts + fh][:][:, 0:512]
                    mms.append(MM(ap_out, hstg[:, hch, ts * 128:(ts + 1) * 128],
                                  w2res[:, hch, fh * 512:(fh + 1) * 512],
                                  first, last))
                tr.group("pe", mms, reads=[("hstg", hch), ("w2r", hch // 8)],
                         writes=[PBANK(2 * ts), PBANK(2 * ts + 1)])
        for ts in range(TSN):
            t = tb * TSN + ts
            for fh in range(2):
                bk = 2 * ts + fh
                dst = acc[:, t, fh * 512:(fh + 1) * 512]
                areg = ("acc", t, fh)
                if sweep == 0:
                    scol = oh1s[:, t, e:e + 1]
                    screg = ("oh1s", t)
                else:
                    scol = w2sc[:, t, e:e + 1]
                    screg = ("w2sc", t)
                if sweep == 0 and e == 0:
                    tr.op("dve", TS(dst, pb[bk][:][:, 0:512], scol, Alu.mult),
                          reads=[PBANK(bk), screg], writes=[areg])
                else:
                    tr.op("dve", STT(dst, pb[bk][:][:, 0:512], scol, dst,
                                     Alu.mult, Alu.add),
                          reads=[PBANK(bk), screg, areg], writes=[areg])
        prefetch()
        drain(dq[3])

    def emit_b2_unit(mp, sweep, tb, maskT, mkey):
        mtiles = [(mkey, tb * TSN + i) for i in range(TSN)]
        for ts in range(TSN):
            t = tb * TSN + ts
            tok0 = tb * TB + ts * 128
            for fh in range(2):
                bk = 2 * ts + fh
                ap_out = pb[bk][:][:, 0:512]
                mms = [MM(ap_out, maskT[:, tok0:tok0 + 128],
                          b2hi[:, fh * 512:(fh + 1) * 512], True, False),
                       MM(ap_out, maskT[:, tok0:tok0 + 128],
                          b2lo[:, fh * 512:(fh + 1) * 512], False, True)]
                tr.group("pe", mms, reads=mtiles + [("b2hi",), ("b2lo",)],
                         writes=[PBANK(bk)])
                dst = acc[:, t, fh * 512:(fh + 1) * 512]
                areg = ("acc", t, fh)
                if sweep == 0:
                    tr.op("dve", TT(dst, pb[bk][:][:, 0:512], dst, Alu.add),
                          reads=[PBANK(bk), areg], writes=[areg])
                else:
                    tr.op("dve", STT(dst, pb[bk][:][:, 0:512], v2s[:, t:t + 1],
                                     dst, Alu.mult, Alu.add),
                          reads=[PBANK(bk), ("v2s", t), areg], writes=[areg])

    def emit_p3_tile(mp, t):
        par = t % 2
        gt = mp * NT + t
        xreg = ("xls", par)
        tr.dma("sync", "xin0" if par == 0 else "xin1", xls[:, par, :], x_t[gt], writes=[xreg])
        xnreg = ("scA", par)
        tr.op("act", ACTV(scA[:, par, :], xls[:, par, :], Act.Identity,
                          bias=lnst[:, t, 0:1], scale=lnst[:, t, 1:2]),
              reads=[xreg, ("lnst", t)], writes=[xnreg])
        rall = [("acc", t, 0), ("acc", t, 1)]
        tr.op("dve", STT(scD[:, 1, :], acc[:, t, :], ALPHA, scA[:, par, :],
                         Alu.mult, Alu.add),
              reads=rall + [xnreg], writes=[("bst",)])
        tr.op("dve", STT(acc[:, t, :], acc[:, t, :], v1s[:, t:t + 1], scA[:, par, :],
                         Alu.mult, Alu.add),
              reads=rall + [xnreg, ("v1s", t)], writes=rall)
        for dc in range(DCH):
            bk = 6 + (dc % 2)
            tr.group("pe", [TRP(pb[bk][:][:, 0:128],
                                scD[:, 1, dc * 128:(dc + 1) * 128], identf[:])],
                     reads=[("bst",), ("identf",)], writes=[PBANK(bk)])
            tr.op("act", ACTV(xnT[:, dc, t * 128:(t + 1) * 128], pb[bk][:][:, 0:128],
                              Act.Copy),
                  reads=[PBANK(bk)], writes=[("xnT", dc, t)])
        tr.op("dve", TS(g8d[:], w2sc[:, t, :], 0.0, Alu.is_gt),
              reads=[("w2sc", t)], writes=[("g8d",)])
        tr.group("pe", [TRP(pb[4][:][0:E, 0:128], g8d[:], identf[:])],
                 reads=[("g8d",), ("identf",)], writes=[PBANK(4)])
        tr.op("act", ACTV(mT[:, t * 128:(t + 1) * 128], pb[4][:][0:E, 0:128],
                          Act.Copy),
              reads=[PBANK(4)], writes=[("mT", t)])

    def emit_epilogue_pre():
        tr.dma("gp", "w2c0", wos, wo_d.rearrange("(c p) f -> p c f", p=128),
               writes=[("wos",)])
        tr.dma("sync", "cst", scD[0:1, 0, :], gamma_d, writes=[("grow",), ("body",)])
        tr.dma("sync", "cst", scD[0:1, 1, :], beta_d, writes=[("brow",), ("bst",)])
        for src_key, dstap, dkey in ((("grow",), grep, "grep"), (("brow",), brep, "brep")):
            row = scD[0:1, 0, :] if dkey == "grep" else scD[0:1, 1, :]
            for half in range(2):
                tr.group("pe", [MM(pb[4 + half][:][:, 0:512], ones1f[:],
                                   row[:, half * 512:(half + 1) * 512], True, True)],
                         reads=[src_key, ("onesf",)], writes=[PBANK(4 + half)])
                tr.op("dve", CPY(dstap[:, half * 512:(half + 1) * 512],
                                 pb[4 + half][:][:, 0:512]),
                      reads=[PBANK(4 + half)], writes=[(dkey,)])

    def emit_epilogue_tile(mp, t):
        par = t % 2
        gt = mp * NT + t
        rall = [("acc", t, 0), ("acc", t, 1)]
        if dbg:
            dbg_dump(dbg_d["z_dbg"].rearrange("(t p) d -> t p d", p=128)[gt],
                     acc[:, t, :], rall)
        mu, rr = scS[:, 0:1], scS[:, 1:2]
        ln_stats(acc[:, t, :], rall, mu, rr)
        tr.op("act", ACTV(scA[:, 0, :], acc[:, t, :], Act.Identity, bias=mu, scale=rr),
              reads=rall + [("mu",), ("var",)], writes=[("ep0",)])
        tr.op("dve", TT(scA[:, 0, :], scA[:, 0, :], grep, Alu.mult),
              reads=[("ep0",), ("grep",)], writes=[("ep0",)])
        tr.op("dve", TT(scA[:, 0, :], scA[:, 0, :], brep, Alu.add),
              reads=[("ep0",), ("brep",)], writes=[("ep0",)])
        tr.op("act", ACTV(scA[:, 1, :], scA[:, 0, :], Act.Exp, scale=-1.0),
              reads=[("ep0",)], writes=[("ep1",)])
        tr.op("dve", TS(scA[:, 1, :], scA[:, 1, :], 1.0, Alu.add),
              reads=[("ep1",)], writes=[("ep1",)])
        tr.op("dve", RECIP(scA[:, 1, :], scA[:, 1, :]), reads=[("ep1",)],
              writes=[("ep1",)])
        tr.op("dve", TT(scA[:, 1, :], scA[:, 1, :], scA[:, 0, :], Alu.mult),
              reads=[("ep1",), ("ep0",)], writes=[("ep1",)])
        abuf = t % 2
        tr.dma("sync", "at", atld[:, abuf, :], at_d[gt], reads=[("atD", gt)], writes=[("atld", abuf)])
        mms = []
        for hc in range(DCH):
            for nh in range(2):
                mms.append(MM(pb[nh][:][:, 0:512],
                              atld[:, abuf, hc * 128:(hc + 1) * 128],
                              wos[:, hc, nh * 512:(nh + 1) * 512], hc == 0, False))
        for nh in range(2):
            mms.append(MM(pb[nh][:][:, 0:512], sel4[:, 384:512],
                          bqkvo_s[:, nh * 512:(nh + 1) * 512], False, True))
        tr.group("pe", mms, reads=[("atld", abuf), ("wos",), ("sel4",), ("bqkvo",)],
                 writes=[PBANK(0), PBANK(1)])
        xreg = ("xls", par)
        tr.dma("sync", "xin0" if par == 0 else "xin1", xls[:, par, :], x_t[gt], writes=[xreg])
        for nh in range(2):
            sl = slice(nh * 512, (nh + 1) * 512)
            tr.op("dve", TT(scD[:, 1, sl], pb[nh][:][:, 0:512], scA[:, 1, sl],
                            Alu.mult),
                  reads=[PBANK(nh), ("ep1",)], writes=[("bst",)])
        tr.op("dve", TT(scD[:, 1, :], scD[:, 1, :], xls[:, par, :], Alu.add),
              reads=[("bst",), xreg], writes=[("bst",)])
        tr.dma("sync", "out", out_t[gt], scD[:, 1, :], reads=[("bst",)])

    PH = int(os.environ.get("KR_PHASES", "9"))
    for mp in range(MP):
        for t in range(NT):
            emit_p0_tile(mp, t)
        tr.barrier()
        if PH < 2:
            continue

        qkv_done = [0]
        spc[0] = 0
        it_ptr = [0]
        iters = []
        for t in range(NT):
            for h in range(NH):
                iters.append((t, h, False))
            iters.append((t, None, True))

        def need_pair(t):
            while qkv_done[0] <= t:
                emit_qkv_pair(mp, qkv_done[0])
                qkv_done[0] += 2

        def drain_attn(k):
            for _ in range(k):
                if it_ptr[0] >= len(iters):
                    return
                t, h, post = iters[it_ptr[0]]
                need_pair(t)
                if post:
                    emit_attn_post(mp, t)
                else:
                    emit_attn_iter(mp, t, h)
                it_ptr[0] += 1

        def drain_rest():
            drain_attn(len(iters))

        emit_w2_load(0)
        emit_w1_dma(0, 0)
        emit_w1_dma(0, 1)
        for e in range(E):
            for tb in range(NTB):
                last = tb == NTB - 1
                e_next = (e + 1) if last else e
                if e_next < E:
                    def pf(e=e, e_next=e_next, last=last):
                        if last:
                            emit_w2_load(e_next)
                        emit_w1_dma(e_next, 0)
                        emit_w1_dma(e_next, 1)
                else:
                    pf = lambda: None
                emit_moe_unit(mp, 0, e, tb, drain=drain_attn, prefetch=pf)
        for tb in range(NTB):
            emit_b2_unit(mp, 0, tb, mT, "mT")
        drain_attn(max(0, (NT - 2) * (NH + 1) - it_ptr[0]))
        need_pair(NT - 1)
        if PH < 4:
            drain_rest()
            tr.barrier()
            continue
        if dbg:
            for t in range(NT):
                dbg_dump(dbg_d["eo1_dbg"].rearrange("(t p) d -> t p d", p=128)[mp * NT + t],
                         acc[:, t, :], [("acc", t, 0), ("acc", t, 1)])
        for t in range(NT):
            emit_p3_tile(mp, t)
            drain_attn(1)
        if PH < 5:
            drain_rest()
            tr.barrier()
            continue
        emit_w2_load(0)
        emit_w1_dma(0, 0)
        emit_w1_dma(0, 1)
        for e in range(E):
            for tb in range(NTB):
                last = tb == NTB - 1
                e_next = (e + 1) if last else e
                if e_next < E:
                    def pf(e=e, e_next=e_next, last=last):
                        if last:
                            emit_w2_load(e_next)
                        emit_w1_dma(e_next, 0)
                        emit_w1_dma(e_next, 1)
                else:
                    pf = lambda: None
                emit_moe_unit(mp, 1, e, tb, drain=drain_attn, prefetch=pf,
                              dq=(1, 1, 1, 0))
        for tb in range(NTB):
            emit_b2_unit(mp, 1, tb, mT, "mT")
        drain_rest()
        tr.barrier()
        if PH < 6:
            continue
        emit_epilogue_pre()
        for t in range(NT):
            emit_epilogue_tile(mp, t)
        tr.barrier()

    if tr.dma_cnt["out"] > 0:
        tr.stream["sync"].append(
            lambda eng, sem=tr.dma_sem["out"], v=tr.dma_cnt["out"]: eng.wait_ge(sem, v))

    with nc.Block() as block:
        @block.tensor
        def _(eng):
            for f in tr.stream["pe"]:
                f(eng)

        @block.vector
        def _(eng):
            for f in tr.stream["dve"]:
                f(eng)

        @block.scalar
        def _(eng):
            for f in tr.stream["act"]:
                f(eng)

        @block.gpsimd
        def _(eng):
            for f in tr.stream["gp"]:
                f(eng)

        @block.sync
        def _(eng):
            for f in tr.stream["sync"]:
                f(eng)

    return nc, es


def prep_inputs(inputs, TOK):
    f32 = lambda a: np.ascontiguousarray(np.asarray(a), dtype=np.float32)
    Wq, Wk, Wv = (f32(inputs["L0_attn_Wq"]), f32(inputs["L0_attn_Wk"]),
                  f32(inputs["L0_attn_Wv"]))
    wqkv = np.concatenate([Wq, Wk, Wv], axis=1)
    W1, W2 = f32(inputs["L0_ffn_W1"]), f32(inputs["L0_ffn_W2"])
    wmoe = np.empty((E, HCH, 128, 2048), dtype=np.float32)
    for e in range(E):
        for hc in range(HCH):
            w1p = W1[e][:, hc * 128:(hc + 1) * 128].reshape(DCH, 128, 128)
            wmoe[e, hc, :, 0:1024] = w1p.transpose(1, 0, 2).reshape(128, 1024)
            wmoe[e, hc, :, 1024:2048] = W2[e][hc * 128:(hc + 1) * 128, :]
    bqkvo = np.stack([f32(inputs["L0_attn_bq"]), f32(inputs["L0_attn_bk"]),
                      f32(inputs["L0_attn_bv"]), f32(inputs["L0_attn_bo"])], axis=0)
    b1 = f32(inputs["L0_ffn_b1"])
    b1p = b1.reshape(E, HCH, 128).transpose(2, 0, 1).reshape(128, E * HCH)
    cm = np.zeros((4, 16, 16), dtype=np.float32)
    for q in range(16):
        cm[:, q, :q + 1] = 1.0
    cmask = np.broadcast_to(cm.reshape(1, 1024), (128, 1024)).astype(ml_dtypes.bfloat16)
    shared = {
        "wqkv": wqkv,
        "gw": f32(inputs["L0_ffn_gate_W"]),
        "gb": f32(inputs["L0_ffn_gate_b"]).reshape(E, 1),
        "wmoe": wmoe,
        "wo": f32(inputs["L0_attn_Wo"]),
        "bqkvo": bqkvo,
        "b1p": np.ascontiguousarray(b1p),
        "b2": f32(inputs["L0_ffn_b2"]),
        "gamma": f32(inputs["L0_ffn_ln_gamma"]).reshape(1, D),
        "beta": f32(inputs["L0_ffn_ln_beta"]).reshape(1, D),
        "identf": np.eye(128, dtype=np.float32),
        "identb": np.eye(128, dtype=ml_dtypes.bfloat16),
        "ones1b": np.ones((1, 128), dtype=ml_dtypes.bfloat16),
        "ones1f": np.ones((1, 128), dtype=np.float32),
        "cmask": np.ascontiguousarray(cmask),
        "fconst": np.tile(np.asarray([FCONST_VALS], dtype=np.float32), (128, 1)),
        "sel4": np.ascontiguousarray(
            np.repeat(np.eye(4, dtype=ml_dtypes.bfloat16), 128, axis=1)),
    }
    x = f32(inputs["L0_attn_x"]).reshape(-1, D)
    xs = [np.ascontiguousarray(x[i * TOK:(i + 1) * TOK]) for i in range(N_CORES)]
    return shared, xs


def kernel(**inputs):
    x = np.asarray(inputs["L0_attn_x"])
    n_tok = x.shape[0] * x.shape[1]
    TOK = n_tok // N_CORES
    shared, xs = prep_inputs(inputs, TOK)
    nc, es = build_program(TOK, dbg=False)
    try:
        in_maps = [dict(shared, x=xs[i]) for i in range(N_CORES)]
        res = run_bass_kernel_spmd(nc, in_maps, core_ids=list(range(N_CORES)))
    finally:
        es.close()
    out = np.concatenate([res.results[i]["out"] for i in range(N_CORES)], axis=0)
    return out.reshape(x.shape).astype(np.float32)


# version 18 (final)
# speedup vs baseline: 1.0334x; 1.0204x over previous
import os
import numpy as np
import ml_dtypes
from contextlib import ExitStack

import concourse.bass as bass
import concourse.mybir as mybir
from concourse.bass_utils import run_bass_kernel_spmd

F32 = mybir.dt.float32
BF16 = mybir.dt.bfloat16
Alu = mybir.AluOpType
Act = mybir.ActivationFunctionType
AX = mybir.AxisListType

N_CORES = 8
D = 1024
NH = 16
C = 64
E = 8
HF = 2048
DCH = D // 128
HCH = HF // 128
LN_EPS = 1e-5
ALPHA = 0.5
TB = 512
TSN = TB // 128
FCONST_VALS = (1e-5, 1.0 / 1024, -0.5, 0.125, -1.0, 1.0, 0.0, 2.0)


def TT(out, a, b, op):
    return lambda e, out=out, a=a, b=b, op=op: e.tensor_tensor(out, a, b, op)

def STT(out, in0, s, in1, op0, op1):
    return lambda e, out=out, in0=in0, s=s, in1=in1, op0=op0, op1=op1: \
        e.scalar_tensor_tensor(out, in0, s, in1, op0, op1)

def TS(out, a, s, op):
    return lambda e, out=out, a=a, s=s, op=op: e.tensor_scalar(out, a, s, None, op)

def CPY(out, a):
    return lambda e, out=out, a=a: e.tensor_copy(out, a)

def ACTV(out, a, f, bias=0.0, scale=1.0):
    return lambda e, out=out, a=a, f=f, bias=bias, scale=scale: \
        e.activation(out, a, f, bias=bias, scale=scale)

def RED(out, a, op, neg=False, absv=False):
    return lambda e, out=out, a=a, op=op, neg=neg, absv=absv: \
        e.tensor_reduce(out, a, AX.X, op, negate=neg, apply_absolute_value=absv)

def AMR(out, accum, in0, in1, scale, bias):
    return lambda e, out=out, accum=accum, in0=in0, in1=in1, scale=scale, bias=bias: \
        e.affine_mul_reduce(out=out, accum_out=accum, in0=in0, in1=in1,
                            scale=scale, bias=bias)

def RECIP(out, a):
    return lambda e, out=out, a=a: e.reciprocal(out, a)

def MM(out, l, r, st, sp):
    return lambda e, out=out, l=l, r=r, st=st, sp=sp: \
        e.matmul(out, l, r, start=st, stop=sp)

def TRP(out, a, ident):
    return lambda e, out=out, a=a, ident=ident: e.transpose(out, a, ident)


class Tracker:
    CE = ("pe", "dve", "act", "gp")

    def __init__(self, nc, es):
        self.nc = nc
        self.stream = {e: [] for e in ("pe", "dve", "act", "gp", "sync")}
        self.sem = {e: es.enter_context(nc.semaphore(f"sem_{e}")) for e in self.CE}
        self.cnt = {e: 0 for e in self.CE}
        self.dma_sem = {}
        self.dma_cnt = {}
        self.last_w = {}
        self.readers = {}
        self.hwm = {}

    def new_dma_sem(self, es, name):
        self.dma_sem[name] = es.enter_context(self.nc.semaphore("dq_" + name))
        self.dma_cnt[name] = 0

    def _wait(self, we, src):
        kind, name, cnt = src
        if kind == "d":
            cnt = self.dma_cnt[name]
        key = (we, kind, name)
        if self.hwm.get(key, -1) >= cnt:
            return
        self.hwm[key] = cnt
        sem = self.sem[name] if kind == "e" else self.dma_sem[name]
        self.stream[we].append(lambda eng, sem=sem, cnt=cnt: eng.wait_ge(sem, cnt))

    def _pre(self, we, reads, writes):
        for r in reads:
            lw = self.last_w.get(r)
            if lw is not None:
                self._wait(we, lw)
            if we in ("act", "dve") and isinstance(r, tuple) and r and r[0] == "pb":
                other = "dve" if we == "act" else "act"
                for rd in self.readers.get(r, ()):
                    if rd[0] == "e" and rd[1] == other:
                        self._wait(we, rd)
        for w in writes:
            lw = self.last_w.get(w)
            if lw is not None:
                self._wait(we, lw)
            for rd in self.readers.get(w, ()):
                self._wait(we, rd)

    def _post(self, kind, name, cnt, reads, writes):
        src = (kind, name, cnt)
        for r in reads:
            self.readers.setdefault(r, []).append(src)
        for w in writes:
            self.last_w[w] = src
            self.readers[w] = []

    def op(self, we, emit, reads=(), writes=()):
        self.group(we, [emit], reads, writes)

    def group(self, we, emits, reads=(), writes=()):
        self._pre(we, reads, writes)
        self.cnt[we] += 1
        cnt = self.cnt[we]
        sem = self.sem[we]

        def run(eng, emits=tuple(emits), sem=sem):
            n = len(emits)
            for i, em in enumerate(emits):
                inst = em(eng)
                if i == n - 1:
                    inst.then_inc(sem, 1)

        self.stream[we].append(run)
        self._post("e", we, cnt, reads, writes)

    def dma(self, qe, sem_name, out, in_, reads=(), writes=()):
        self._pre(qe, reads, writes)
        self.dma_cnt[sem_name] += 16
        cnt = self.dma_cnt[sem_name]
        sem = self.dma_sem[sem_name]

        def run(eng, out=out, in_=in_, sem=sem):
            eng.dma_start(out, in_).then_inc(sem, 16)

        self.stream[qe].append(run)
        self._post("d", sem_name, cnt, reads, writes)

    def barrier(self):
        snap = dict(self.cnt)
        dsnap = dict(self.dma_cnt)
        for we in self.CE + ("sync",):
            for se in self.CE:
                if se != we and snap[se] > 0:
                    self._wait(we, ("e", se, snap[se]))
            for dn, c in dsnap.items():
                if c > 0:
                    self._wait(we, ("d", dn, c))


def build_program(TOK, dbg=False):
    MP = 2 if TOK >= 2048 else 1
    MT = TOK // MP
    NT = MT // 128
    NTB = MT // TB

    nc = bass.Bass("TRN2", target_bir_lowering=False)
    es = ExitStack()

    def din(name, shape, dt=F32):
        return nc.dram_tensor(name, list(shape), dt, kind="ExternalInput").ap()

    x_d = din("x", (TOK, D))
    wqkv_d = din("wqkv", (D, 3 * D))
    gw_d = din("gw", (D, E))
    gb_d = din("gb", (E, 1))
    wmoe_d = din("wmoe", (E, HCH, 128, 2048))
    wo_d = din("wo", (D, D))
    bqkvo_d = din("bqkvo", (4, D))
    b1p_d = din("b1p", (128, 128))
    b2_d = din("b2", (E, D))
    gamma_d = din("gamma", (1, D))
    beta_d = din("beta", (1, D))
    identf_d = din("identf", (128, 128))
    ones1b_d = din("ones1b", (1, 128), BF16)
    ones1f_d = din("ones1f", (1, 128))
    cmask_d = din("cmask", (128, 1024), BF16)
    sel4_d = din("sel4", (4, 512), BF16)
    fconst_d = din("fconst", (128, 8))

    out_d = nc.dram_tensor("out", [TOK, D], F32, kind="ExternalOutput").ap()
    at_d = nc.dram_tensor("atspill", [MP * NT, 128, D], BF16, kind="Internal").ap()

    dbg_d = {}
    if dbg:
        for nm, sh in (("xn_dbg", (TOK, D)), ("gate_dbg", (TOK, E)),
                       ("sel_dbg", (TOK, 4)), ("a_dbg", (TOK, D)),
                       ("eo1_dbg", (TOK, D)), ("z_dbg", (TOK, D)),
                       ("qkv_dbg", (3, 128, D)),):
            dbg_d[nm] = nc.dram_tensor(nm, list(sh), F32, kind="ExternalOutput").ap()

    sb = lambda n, s, d=F32: es.enter_context(nc.sbuf_tensor(n, list(s), d))

    xnT = sb("xnT", (128, DCH, MT), BF16)
    acc = sb("acc", (128, NT, D), F32)
    mT = sb("mT", (E, MT), BF16)
    v1s = sb("v1s", (128, NT), F32)
    v2s = sb("v2s", (128, NT), F32)
    oh1s = sb("oh1s", (128, NT, E), F32)
    w2sc = sb("w2sc", (128, NT, E), F32)
    b1s = sb("b1s", (128, 128), F32)
    b2hi = sb("b2hi", (E, D), BF16)
    b2lo = sb("b2lo", (E, D), BF16)
    gbs = sb("gbs", (E, 1), F32)
    bqkvo_s = sb("bqkvo_s", (4, D), BF16)
    sel4 = sb("sel4_s", (4, 512), BF16)
    identf = sb("identf_s", (128, 128), F32)
    ones1b = sb("ones1b_s", (1, 128), BF16)
    ones1f = sb("ones1f_s", (1, 128), F32)
    cmask = sb("cmask_s", (128, 1024), BF16)
    gws = sb("gws", (128, DCH * E), F32)
    fconst = sb("fconst_s", (128, 8), F32)
    sW = sb("sW", (128, 3, 1024), BF16)
    qkvw = sb("qkvw", (128, 2, 2, 1024), BF16)
    qt = sb("qt", (128, 2, D), BF16)
    kt = sb("kt", (128, 2, D), BF16)
    vt = sb("vt", (128, 2, D), BF16)
    sS = sb("sS", (128, 3, 1024), F32)
    sE = sb("sE", (128, 3, 1024), F32)
    sA = sb("sA", (128, D), F32)
    atstg = sb("atstg", (128, D), BF16)
    w1s = sb("w1s", (128, 2, 4, 1024), BF16)
    w2res = sb("w2res", (128, HCH, 1024), BF16)
    hstg = sb("hstg", (128, HCH, TB), BF16)
    xls = sb("xls", (128, 2, D), F32)
    scA = sb("scA", (128, 2, D), F32)
    scD = sb("scD", (128, 2, D), F32)
    scS = sb("scS", (128, 8), F32)
    atst = sb("atst", (128, 2, 32), F32)
    atst2 = sb("atst2", (128, 2, 16), F32)
    lnst = sb("lnst", (128, NT, 2), F32)
    dpart = sb("dpart", (128, C), F32)
    npart = sb("npart", (128, C), F32)
    rtmp = sb("rtmp", (128, 96), F32)
    g8a = sb("g8a", (E, 128), F32)
    g8b = sb("g8b", (128, E), F32)
    g8c = sb("g8c", (128, E), F32)
    g8d = sb("g8d", (128, E), F32)
    mrow = sb("mrow", (128, 2, E), F32)
    dbgbuf = sb("dbgbuf", (128, D), F32) if dbg else None
    wos = w2res[:].rearrange("p a b -> p (a b)")[:, 0:DCH * D].rearrange(
        "p (c f) -> p c f", c=DCH)
    atld = hstg[:].rearrange("p a b -> p (a b)")[:, 0:2 * D].rearrange(
        "p (c f) -> p c f", c=2)
    grep = sS[:].rearrange("p a b -> p (a b)")[:, 0:D]
    brep = sS[:].rearrange("p a b -> p (a b)")[:, D:2 * D]
    b2f = scA[0:E, 0, :]

    pb = [es.enter_context(nc.psum_tensor(f"pb{i}", [128, 512], F32)) for i in range(8)]
    PBANK = lambda i: ("pb", i)

    tr = Tracker(nc, es)
    for s in (["xin0", "xin1", "wq0", "wq1", "cst", "cstg",
               "out", "at", "dbg"] + [f"w1s{i}" for i in range(4)]
              + [f"w2c{i}" for i in range(2)]):
        tr.new_dma_sem(es, s)

    x_t = x_d.rearrange("(t p) d -> t p d", p=128)
    out_t = out_d.rearrange("(t p) d -> t p d", p=128)

    def dbg_dump(dst, src_ap, reads):
        w = src_ap.shape[-1]
        tr.op("act", ACTV(dbgbuf[:, 0:w], src_ap, Act.Copy),
              reads=list(reads), writes=[("dbgB",)])
        tr.dma("sync", "dbg", dst, dbgbuf[:, 0:w], reads=[("dbgB",)])

    for dst, src, key in ((identf[:], identf_d, "identf"),
                          (ones1b[:], ones1b_d, "ones"), (ones1f[:], ones1f_d, "onesf"),
                          (b1s[:], b1p_d, "b1s"), (gbs[:], gb_d, "gbs")):
        tr.dma("sync", "cst", dst, src, writes=[(key,)])
    tr.dma("sync", "cst", gws[:].rearrange("p (c e) -> p c e", c=DCH),
           gw_d.rearrange("(c p) e -> p c e", p=128), writes=[("gws",)])
    tr.dma("gp", "cstg", cmask[:], cmask_d, writes=[("cmask",)])
    tr.dma("sync", "cst", fconst[:], fconst_d, writes=[("fconst",)])
    for _i, _v in enumerate(FCONST_VALS):
        nc.const_aps.register(F32, _v, fconst[:, _i:_i + 1])
    tr.dma("gp", "cstg", bqkvo_s[:], bqkvo_d, writes=[("bqkvo",)])
    tr.dma("sync", "cst", sel4[:], sel4_d, writes=[("sel4",)])
    tr.dma("sync", "cst", b2f, b2_d, writes=[("b2f",)])
    tr.op("act", ACTV(b2hi[:], b2f, Act.Copy), reads=[("b2f",)], writes=[("b2hi",)])
    tr.op("dve", TT(b2lo[:], b2f, b2hi[:], Alu.subtract),
          reads=[("b2f",), ("b2hi",)], writes=[("b2lo",)])
    tr.barrier()

    def ln_stats(xap, xregs, mu, rr):
        tr.op("dve", RED(mu, xap, Alu.add), reads=xregs, writes=[("mu",)])
        tr.op("dve", TS(mu, mu, -1.0 / D, Alu.mult), reads=[("mu",)], writes=[("mu",)])
        tr.op("dve", lambda e, xap=xap, mu=mu, rr=rr: e.scalar_tensor_tensor(
            scD[:, 0, :], xap, mu, xap, Alu.add, Alu.mult, accum_out=rr),
              reads=xregs + [("mu",)], writes=[("var",), ("body",)])
        tr.op("act", ACTV(rr, rr, Act.Ln, bias=LN_EPS, scale=1.0 / D),
              reads=[("var",)], writes=[("var",)])
        tr.op("act", ACTV(rr, rr, Act.Exp, scale=-0.5), reads=[("var",)], writes=[("var",)])
        tr.op("dve", TT(mu, mu, rr, Alu.mult), reads=[("mu",), ("var",)], writes=[("mu",)])

    P0LITE = os.environ.get("KR_P0LITE", "")
    def emit_p0_tile(mp, t):
        par = t % 2
        gt = mp * NT + t
        xreg = ("xls", par)
        tr.dma("sync", "xin0" if par == 0 else "xin1", xls[:, par, :], x_t[gt], writes=[xreg])
        mu, rr = lnst[:, t, 0:1], lnst[:, t, 1:2]
        ln_stats(xls[:, par, :], [xreg], mu, rr)
        tr.op("dve", CPY(scS[:, 0:2], lnst[:, t, :]),
              reads=[("mu",), ("var",)], writes=[("lnst", t)])
        xnreg = ("scA", par)
        tr.op("act", ACTV(scA[:, par, :], xls[:, par, :], Act.Identity, bias=mu, scale=rr),
              reads=[xreg, ("lnst", t)], writes=[xnreg])
        if dbg:
            dbg_dump(dbg_d["xn_dbg"].rearrange("(t p) d -> t p d", p=128)[gt],
                     scA[:, par, :], [xnreg])
        if P0LITE == "nope":
            return
        for dc in range(DCH):
            bk = 6 + (dc % 2)
            tr.group("pe", [TRP(pb[bk][:][:, 0:128],
                                scA[:, par, dc * 128:(dc + 1) * 128], identf[:])],
                     reads=[xnreg, ("identf",)], writes=[PBANK(bk)])
            tr.op("act", ACTV(xnT[:, dc, t * 128:(t + 1) * 128], pb[bk][:][:, 0:128],
                              Act.Copy),
                  reads=[PBANK(bk)], writes=[("xnT", dc, t)])
            xf = scD[:, 1, (dc % 2) * 128:(dc % 2) * 128 + 128]
            tr.op("act", ACTV(xf, pb[bk][:][:, 0:128], Act.Copy),
                  reads=[PBANK(bk)], writes=[("xf32", dc % 2)])
            tr.group("pe", [MM(pb[5][:][0:E, 0:128], gws[:, dc * E:(dc + 1) * E],
                               xf, dc == 0, dc == DCH - 1)],
                     reads=[("xf32", dc % 2), ("gws",)], writes=[PBANK(5)])
        if P0LITE in ("nosmall", "nogate"):
            return
        tr.op("act", ACTV(g8a[:], pb[5][:][0:E, 0:128], Act.Identity, bias=gbs[:]),
              reads=[PBANK(5), ("gbs",)], writes=[("g8a",)])
        tr.group("pe", [TRP(pb[4][:][:, 0:E], g8a[:], identf[0:E, 0:E])],
                 reads=[("g8a",), ("identf",)], writes=[PBANK(4)])
        tr.op("dve", CPY(g8b[:], pb[4][:][:, 0:E]), reads=[PBANK(4)], writes=[("g8b",)])
        if dbg:
            dbg_dump(dbg_d["gate_dbg"].rearrange("(t p) e -> t p e", p=128)[gt],
                     g8b[:], [("g8b",)])
        mg, sg = scS[:, 2:3], scS[:, 3:4]
        tr.op("dve", RED(mg, g8b[:], Alu.max, neg=True), reads=[("g8b",)], writes=[("mg",)])
        tr.op("act", ACTV(g8c[:], g8b[:], Act.Exp, bias=mg),
              reads=[("g8b",), ("mg",)], writes=[("g8c",)])
        tr.op("dve", RED(sg, g8c[:], Alu.add), reads=[("g8c",)], writes=[("sg",)])
        tr.op("dve", RECIP(sg, sg), reads=[("sg",)], writes=[("sg",)])
        tr.op("dve", TS(g8c[:], g8c[:], sg, Alu.mult), reads=[("g8c",), ("sg",)],
              writes=[("g8c",)])
        va, vb = v1s[:, t:t + 1], v2s[:, t:t + 1]
        tr.op("dve", RED(va, g8c[:], Alu.max), reads=[("g8c",)], writes=[("v1s", t)])
        tr.op("dve", TS(mrow[:, 0, :], g8c[:], va, Alu.is_equal),
              reads=[("g8c",), ("v1s", t)], writes=[("mrow", 0)])
        tr.op("dve", STT(g8d[:], mrow[:, 0, :], -1e30, g8c[:], Alu.mult, Alu.add),
              reads=[("mrow", 0), ("g8c",)], writes=[("g8d",)])
        tr.op("dve", RED(vb, g8d[:], Alu.max), reads=[("g8d",)], writes=[("v2s", t)])
        tr.op("dve", TS(mrow[:, 1, :], g8d[:], vb, Alu.is_equal),
              reads=[("g8d",), ("v2s", t)], writes=[("mrow", 1)])
        tr.op("dve", CPY(oh1s[:, t, :], mrow[:, 0, :]),
              reads=[("mrow", 0)], writes=[("oh1s", t)])
        tr.op("dve", TS(w2sc[:, t, :], mrow[:, 1, :], vb, Alu.mult),
              reads=[("mrow", 1), ("v2s", t)], writes=[("w2sc", t)])
        tr.group("pe", [TRP(pb[4][:][0:E, 0:128], mrow[:, 0, :], identf[:])],
                 reads=[("mrow", 0), ("identf",)], writes=[PBANK(4)])
        tr.op("act", ACTV(mT[:, t * 128:(t + 1) * 128], pb[4][:][0:E, 0:128],
                          Act.Copy),
              reads=[PBANK(4)], writes=[("mT", t)])
        if dbg:
            tr.op("dve", CPY(rtmp[:, 0:1], va), reads=[("v1s", t)], writes=[("seldmp",)])
            tr.op("dve", CPY(rtmp[:, 1:2], vb), reads=[("v2s", t)], writes=[("seldmp",)])
            tr.op("dve", RED(rtmp[:, 2:3], g8c[:], Alu.add), reads=[("g8c",)],
                  writes=[("seldmp",)])
            tr.op("dve", RED(rtmp[:, 3:4], g8d[:], Alu.max), reads=[("g8d",)],
                  writes=[("seldmp",)])
            dbg_dump(dbg_d["sel_dbg"].rearrange("(t p) e -> t p e", p=128)[gt],
                     rtmp[:, 0:4], [("seldmp",)])

    def emit_qkv_pair(mp, t0):
        tiles = [t0, t0 + 1] if t0 + 1 < NT else [t0]
        dsts = ((qt, "qt"), (kt, "kt"), (vt, "vt"))
        for proj in range(3):
            for qtr in range(4):
                sl = qtr % 2
                tr.dma("gp", f"wq{sl}", qkvw[:, sl, :, :],
                       wqkv_d[qtr * 256:(qtr + 1) * 256,
                              proj * D:(proj + 1) * D].rearrange(
                           "(c p) f -> p c f", p=128),
                       writes=[("qkvw", sl)])
                for ti, t in enumerate(tiles):
                    mms = []
                    for dq in range(2):
                        dc = qtr * 2 + dq
                        for nh in range(2):
                            wv = qkvw[:, sl, dq, nh * 512:(nh + 1) * 512]
                            mms.append(MM(pb[2 + 2 * ti + nh][:][:, 0:512],
                                          xnT[:, dc, t * 128:(t + 1) * 128], wv,
                                          dc == 0, False))
                    if qtr == 3:
                        for nh in range(2):
                            mms.append(MM(pb[2 + 2 * ti + nh][:][:, 0:512],
                                          sel4[:, proj * 128:(proj + 1) * 128],
                                          bqkvo_s[:, nh * 512:(nh + 1) * 512],
                                          False, True))
                    rds = [("xnT", qtr * 2 + dq, t) for dq in range(2)] + \
                          [("qkvw", sl), ("sel4",), ("bqkvo",)]
                    tr.group("pe", mms, reads=rds,
                             writes=[PBANK(2 + 2 * ti), PBANK(3 + 2 * ti)])
            for ti, t in enumerate(tiles):
                par = t % 2
                dst, key = dsts[proj]
                scl = 0.125 if proj == 0 else 1.0
                for nh in range(2):
                    tr.op("act", ACTV(dst[:, par, nh * 512:(nh + 1) * 512],
                                      pb[2 + 2 * ti + nh][:][:, 0:512], Act.Copy,
                                      scale=scl),
                          reads=[PBANK(2 + 2 * ti + nh)], writes=[(key, par)])
                if dbg and t == 0:
                    dbg_dump(dbg_d["qkv_dbg"][proj], dst[:, par, :], [(key, par)])
        for t in tiles:
            par = t % 2
            qv3 = qt[:, par, :].rearrange("p (h c) -> p h c", h=NH)
            kv3 = kt[:, par, :].rearrange("p (h c) -> p h c", h=NH)
            tr.op("dve", RED(atst[:, par, 0:16], qv3, Alu.max, absv=True),
                  reads=[("qt", par)], writes=[("qmax", par)])
            tr.op("dve", RED(atst[:, par, 16:32], kv3, Alu.max, absv=True, neg=True),
                  reads=[("kt", par)], writes=[("kmax", par)])
            tr.op("dve", TT(atst2[:, par, :], atst[:, par, 0:16], atst[:, par, 16:32],
                            Alu.mult),
                  reads=[("qmax", par), ("kmax", par)], writes=[("ngcs", par)])

    spc = [0]
    def emit_attn_iter(mp, t, h):
        par = t % 2
        qs = qt[:, par, h * C:(h + 1) * C]
        ks = kt[:, par, h * C:(h + 1) * C]
        vs = vt[:, par, h * C:(h + 1) * C]
        rq, rk, rv = ("qt", par), ("kt", par), ("vt", par)
        ngc = atst2[:, par, h:h + 1]
        rngc = ("ngcs", par)

        WID = {"B": 1024, 1: 256, 2: 512, 3: 768}
        base = spc[0]
        SP = {"B": base % 3, 1: (base + 1) % 3, 2: (base + 2) % 3, 3: (base + 3) % 3}
        spc[0] = base + 4

        def qkv_aps(tag):
            if tag == "B":
                qv = qs.rearrange("p (b q) -> p b q", b=4).unsqueeze(3) \
                    .broadcast_to([128, 4, 16, 16])
                kv = ks.rearrange("p (b k) -> p b k", b=4).unsqueeze(2) \
                    .broadcast_to([128, 4, 16, 16])
                vv = vs.rearrange("p (b k) -> p b k", b=4).unsqueeze(2) \
                    .broadcast_to([128, 4, 16, 16])
            else:
                j = tag
                w = 16 * j
                qv = qs[:, 16 * j:16 * j + 16].unsqueeze(2).broadcast_to([128, 16, w])
                kv = ks[:, 0:w].unsqueeze(1).broadcast_to([128, 16, w])
                vv = vs[:, 0:w].unsqueeze(1).broadcast_to([128, 16, w])
            return qv, kv, vv

        def sview(buf, sp, tag):
            w = WID[tag]
            flat = buf[:, sp, 0:w]
            if tag == "B":
                return flat.rearrange("p (b q k) -> p b q k", b=4, q=16), flat
            return flat.rearrange("p (q k) -> p q k", q=16), flat

        def smult(tag):
            sp = SP[tag]
            qv, kv, _ = qkv_aps(tag)
            o3, _ = sview(sS, sp, tag)
            tr.op("gp", TT(o3, qv, kv, Alu.mult), reads=[rq, rk],
                  writes=[("sS", sp)])

        def expb(tag):
            sp = SP[tag]
            _, sflat = sview(sS, sp, tag)
            _, eflat = sview(sE, sp, tag)
            tr.op("act", ACTV(eflat, sflat, Act.Exp, bias=ngc),
                  reads=[("sS", sp), rngc], writes=[("sE", sp)])

        def maskb():
            sp = SP["B"]
            tr.op("dve", TT(sE[:, sp, 0:1024], sE[:, sp, 0:1024], cmask[:], Alu.mult),
                  reads=[("sE", sp), ("cmask",)], writes=[("sE", sp)])

        def denom(tag):
            sp = SP[tag]
            e3, _ = sview(sE, sp, tag)
            if tag == "B":
                tr.op("dve", RED(dpart[:].rearrange("p (b q) -> p b q", b=4), e3,
                                 Alu.add),
                      reads=[("sE", sp)], writes=[("dpart",)])
            else:
                j = tag
                tr.op("dve", RED(rtmp[:, 16 * (j - 1):16 * j], e3, Alu.add),
                      reads=[("sE", sp)], writes=[("rtd", j)])

        def wview(sp, tag):
            w = WID[tag]
            flat = sW[:, sp, 0:w]
            if tag == "B":
                return flat.rearrange("p (b q k) -> p b q k", b=4, q=16)
            return flat.rearrange("p (q k) -> p q k", q=16)

        def vmult(tag):
            sp = SP[tag]
            e3, _ = sview(sE, sp, tag)
            _, _, vv = qkv_aps(tag)
            w3 = wview(sp, tag)
            tr.op("gp", TT(w3, e3, vv, Alu.mult),
                  reads=[("sE", sp), rv], writes=[("sW", sp)])

        def numer(tag):
            sp = SP[tag]
            w3 = wview(sp, tag)
            if tag == "B":
                tr.op("dve", RED(npart[:].rearrange("p (b q) -> p b q", b=4), w3,
                                 Alu.add),
                      reads=[("sW", sp)], writes=[("npart",)])
            else:
                j = tag
                tr.op("dve", RED(rtmp[:, 48 + 16 * (j - 1):48 + 16 * j], w3, Alu.add),
                      reads=[("sW", sp)], writes=[("rtn", j)])

        smult("B"); smult(1)
        expb("B")
        maskb(); denom("B")
        smult(2)
        vmult("B")
        expb(1)
        numer("B"); denom(1)
        smult(3)
        vmult(1)
        expb(2)
        numer(1); denom(2)
        vmult(2)
        expb(3)
        numer(2); denom(3)
        vmult(3)
        numer(3)
        tr.op("dve", TT(dpart[:, 16:64], dpart[:, 16:64], rtmp[:, 0:48], Alu.add),
              reads=[("rtd", 1), ("rtd", 2), ("rtd", 3), ("dpart",)],
              writes=[("dpart",)])
        tr.op("dve", TT(npart[:, 16:64], npart[:, 16:64], rtmp[:, 48:96], Alu.add),
              reads=[("rtn", 1), ("rtn", 2), ("rtn", 3), ("npart",)],
              writes=[("npart",)])
        tr.op("dve", RECIP(dpart[:], dpart[:]), reads=[("dpart",)], writes=[("dpart",)])
        tr.op("dve", TT(sA[:, h * C:(h + 1) * C], npart[:], dpart[:], Alu.mult),
              reads=[("npart",), ("dpart",)], writes=[("sA", h)])

    def emit_attn_post(mp, t):
        gt = mp * NT + t
        rall = [("sA", h) for h in range(NH)]
        if dbg:
            dbg_dump(dbg_d["a_dbg"].rearrange("(t p) d -> t p d", p=128)[gt],
                     sA[:], rall)
        for dc in range(DCH):
            bk = 2 + (dc % 2)
            tr.group("pe", [TRP(pb[bk][:][:, 0:128], sA[:, dc * 128:(dc + 1) * 128],
                                identf[:])],
                     reads=rall + [("identf",)], writes=[PBANK(bk)])
            tr.op("act", ACTV(atstg[:, dc * 128:(dc + 1) * 128],
                              pb[bk][:][:, 0:128], Act.Copy),
                  reads=[PBANK(bk)], writes=[("atstg",)])
        tr.dma("sync", "at", at_d[gt], atstg[:], reads=[("atstg",)], writes=[("atD", gt)])

    def emit_w2_load(e, half=None):
        for h in ((0, 1) if half is None else (half,)):
            tr.dma("gp", f"w2c{h}", w2res[:, h * 8:(h + 1) * 8, :],
                   wmoe_d[e, h * 8:(h + 1) * 8, :, 1024:2048].rearrange(
                       "c p f -> p c f"),
                   writes=[("w2r", h)])

    def emit_w1_dma(e, q):
        sl = q % 2
        tr.dma("gp", "w1s%d" % sl, w1s[:, sl, :, :],
               wmoe_d[e, q * 4:(q + 1) * 4, :, 0:1024].rearrange("c p f -> p c f"),
               writes=[("w1s", sl)])

    def emit_moe_unit(mp, sweep, e, tb, drain=lambda k: None, prefetch=lambda: None,
                      dq=(1, 1, 2, 2)):
        wsel = [("xnT", dc, tb * TSN + ts) for dc in range(DCH) for ts in range(TSN)]

        def hgroup(hch):
            sl = (hch // 4) % 2
            hb = hch % 2
            w1c = w1s[:, sl, hch % 4, :]
            mms = [MM(pb[hb][:][:, 0:TB], w1c[:, dc * 128:(dc + 1) * 128],
                      xnT[:, dc, tb * TB:(tb + 1) * TB], dc == 0, dc == DCH - 1)
                   for dc in range(DCH)]
            tr.group("pe", mms, reads=wsel + [("w1s", sl)], writes=[PBANK(hb)])
            tr.op("act", ACTV(hstg[:, hch, :], pb[hb][:][:, 0:TB], Act.Relu,
                              bias=b1s[:, e * 16 + hch:e * 16 + hch + 1]),
                  reads=[PBANK(hb), ("b1s",)], writes=[("hstg", hch)])

        for hch in range(8):
            hgroup(hch)
        drain(1)
        emit_w1_dma(e, 2)
        emit_w1_dma(e, 3)
        drain(1)
        for hch in range(8, HCH):
            hgroup(hch)
        drain(dq[2])
        for hch in range(HCH):
            first, last = hch == 0, hch == HCH - 1
            for ts in range(TSN):
                mms = []
                for fh in range(2):
                    ap_out = pb[2 * ts + fh][:][:, 0:512]
                    mms.append(MM(ap_out, hstg[:, hch, ts * 128:(ts + 1) * 128],
                                  w2res[:, hch, fh * 512:(fh + 1) * 512],
                                  first, last))
                tr.group("pe", mms, reads=[("hstg", hch), ("w2r", hch // 8)],
                         writes=[PBANK(2 * ts), PBANK(2 * ts + 1)])
        for ts in range(TSN):
            t = tb * TSN + ts
            for fh in range(2):
                bk = 2 * ts + fh
                dst = acc[:, t, fh * 512:(fh + 1) * 512]
                areg = ("acc", t, fh)
                if sweep == 0:
                    scol = oh1s[:, t, e:e + 1]
                    screg = ("oh1s", t)
                else:
                    scol = w2sc[:, t, e:e + 1]
                    screg = ("w2sc", t)
                if sweep == 0 and e == 0:
                    tr.op("dve", TS(dst, pb[bk][:][:, 0:512], scol, Alu.mult),
                          reads=[PBANK(bk), screg], writes=[areg])
                else:
                    tr.op("dve", STT(dst, pb[bk][:][:, 0:512], scol, dst,
                                     Alu.mult, Alu.add),
                          reads=[PBANK(bk), screg, areg], writes=[areg])
        prefetch()
        drain(dq[3])

    def emit_b2_unit(mp, sweep, tb, maskT, mkey):
        mtiles = [(mkey, tb * TSN + i) for i in range(TSN)]
        for ts in range(TSN):
            t = tb * TSN + ts
            tok0 = tb * TB + ts * 128
            for fh in range(2):
                bk = 2 * ts + fh
                ap_out = pb[bk][:][:, 0:512]
                mms = [MM(ap_out, maskT[:, tok0:tok0 + 128],
                          b2hi[:, fh * 512:(fh + 1) * 512], True, False),
                       MM(ap_out, maskT[:, tok0:tok0 + 128],
                          b2lo[:, fh * 512:(fh + 1) * 512], False, True)]
                tr.group("pe", mms, reads=mtiles + [("b2hi",), ("b2lo",)],
                         writes=[PBANK(bk)])
                dst = acc[:, t, fh * 512:(fh + 1) * 512]
                areg = ("acc", t, fh)
                if sweep == 0:
                    tr.op("dve", TT(dst, pb[bk][:][:, 0:512], dst, Alu.add),
                          reads=[PBANK(bk), areg], writes=[areg])
                else:
                    tr.op("dve", STT(dst, pb[bk][:][:, 0:512], v2s[:, t:t + 1],
                                     dst, Alu.mult, Alu.add),
                          reads=[PBANK(bk), ("v2s", t), areg], writes=[areg])

    def emit_p3_tile(mp, t):
        par = t % 2
        gt = mp * NT + t
        xreg = ("xls", par)
        tr.dma("sync", "xin0" if par == 0 else "xin1", xls[:, par, :], x_t[gt], writes=[xreg])
        xnreg = ("scA", par)
        tr.op("act", ACTV(scA[:, par, :], xls[:, par, :], Act.Identity,
                          bias=lnst[:, t, 0:1], scale=lnst[:, t, 1:2]),
              reads=[xreg, ("lnst", t)], writes=[xnreg])
        rall = [("acc", t, 0), ("acc", t, 1)]
        tr.op("dve", STT(scD[:, 1, :], acc[:, t, :], ALPHA, scA[:, par, :],
                         Alu.mult, Alu.add),
              reads=rall + [xnreg], writes=[("bst",)])
        tr.op("dve", STT(acc[:, t, :], acc[:, t, :], v1s[:, t:t + 1], scA[:, par, :],
                         Alu.mult, Alu.add),
              reads=rall + [xnreg, ("v1s", t)], writes=rall)
        for dc in range(DCH):
            bk = 6 + (dc % 2)
            tr.group("pe", [TRP(pb[bk][:][:, 0:128],
                                scD[:, 1, dc * 128:(dc + 1) * 128], identf[:])],
                     reads=[("bst",), ("identf",)], writes=[PBANK(bk)])
            tr.op("act", ACTV(xnT[:, dc, t * 128:(t + 1) * 128], pb[bk][:][:, 0:128],
                              Act.Copy),
                  reads=[PBANK(bk)], writes=[("xnT", dc, t)])
        tr.op("dve", TS(g8d[:], w2sc[:, t, :], 0.0, Alu.is_gt),
              reads=[("w2sc", t)], writes=[("g8d",)])
        tr.group("pe", [TRP(pb[4][:][0:E, 0:128], g8d[:], identf[:])],
                 reads=[("g8d",), ("identf",)], writes=[PBANK(4)])
        tr.op("act", ACTV(mT[:, t * 128:(t + 1) * 128], pb[4][:][0:E, 0:128],
                          Act.Copy),
              reads=[PBANK(4)], writes=[("mT", t)])

    def emit_epilogue_pre():
        tr.dma("gp", "w2c0", wos, wo_d.rearrange("(c p) f -> p c f", p=128),
               writes=[("wos",)])
        tr.dma("sync", "cst", scD[0:1, 0, :], gamma_d, writes=[("grow",), ("body",)])
        tr.dma("sync", "cst", scD[0:1, 1, :], beta_d, writes=[("brow",), ("bst",)])
        for src_key, dstap, dkey in ((("grow",), grep, "grep"), (("brow",), brep, "brep")):
            row = scD[0:1, 0, :] if dkey == "grep" else scD[0:1, 1, :]
            for half in range(2):
                tr.group("pe", [MM(pb[4 + half][:][:, 0:512], ones1f[:],
                                   row[:, half * 512:(half + 1) * 512], True, True)],
                         reads=[src_key, ("onesf",)], writes=[PBANK(4 + half)])
                tr.op("dve", CPY(dstap[:, half * 512:(half + 1) * 512],
                                 pb[4 + half][:][:, 0:512]),
                      reads=[PBANK(4 + half)], writes=[(dkey,)])

    def emit_epilogue_tile(mp, t):
        par = t % 2
        gt = mp * NT + t
        rall = [("acc", t, 0), ("acc", t, 1)]
        if dbg:
            dbg_dump(dbg_d["z_dbg"].rearrange("(t p) d -> t p d", p=128)[gt],
                     acc[:, t, :], rall)
        mu, rr = scS[:, 0:1], scS[:, 1:2]
        ln_stats(acc[:, t, :], rall, mu, rr)
        tr.op("act", ACTV(scA[:, 0, :], acc[:, t, :], Act.Identity, bias=mu, scale=rr),
              reads=rall + [("mu",), ("var",)], writes=[("ep0",)])
        tr.op("dve", TT(scA[:, 0, :], scA[:, 0, :], grep, Alu.mult),
              reads=[("ep0",), ("grep",)], writes=[("ep0",)])
        tr.op("dve", TT(scA[:, 0, :], scA[:, 0, :], brep, Alu.add),
              reads=[("ep0",), ("brep",)], writes=[("ep0",)])
        tr.op("act", ACTV(scA[:, 1, :], scA[:, 0, :], Act.Exp, scale=-1.0),
              reads=[("ep0",)], writes=[("ep1",)])
        tr.op("dve", TS(scA[:, 1, :], scA[:, 1, :], 1.0, Alu.add),
              reads=[("ep1",)], writes=[("ep1",)])
        tr.op("dve", RECIP(scA[:, 1, :], scA[:, 1, :]), reads=[("ep1",)],
              writes=[("ep1",)])
        tr.op("dve", TT(scA[:, 1, :], scA[:, 1, :], scA[:, 0, :], Alu.mult),
              reads=[("ep1",), ("ep0",)], writes=[("ep1",)])
        abuf = t % 2
        tr.dma("sync", "at", atld[:, abuf, :], at_d[gt], reads=[("atD", gt)], writes=[("atld", abuf)])
        mms = []
        for hc in range(DCH):
            for nh in range(2):
                mms.append(MM(pb[nh][:][:, 0:512],
                              atld[:, abuf, hc * 128:(hc + 1) * 128],
                              wos[:, hc, nh * 512:(nh + 1) * 512], hc == 0, False))
        for nh in range(2):
            mms.append(MM(pb[nh][:][:, 0:512], sel4[:, 384:512],
                          bqkvo_s[:, nh * 512:(nh + 1) * 512], False, True))
        tr.group("pe", mms, reads=[("atld", abuf), ("wos",), ("sel4",), ("bqkvo",)],
                 writes=[PBANK(0), PBANK(1)])
        xreg = ("xls", par)
        tr.dma("sync", "xin0" if par == 0 else "xin1", xls[:, par, :], x_t[gt], writes=[xreg])
        for nh in range(2):
            sl = slice(nh * 512, (nh + 1) * 512)
            tr.op("dve", TT(scD[:, 1, sl], pb[nh][:][:, 0:512], scA[:, 1, sl],
                            Alu.mult),
                  reads=[PBANK(nh), ("ep1",)], writes=[("bst",)])
        tr.op("dve", TT(scD[:, 1, :], scD[:, 1, :], xls[:, par, :], Alu.add),
              reads=[("bst",), xreg], writes=[("bst",)])
        tr.dma("sync", "out", out_t[gt], scD[:, 1, :], reads=[("bst",)])

    PH = int(os.environ.get("KR_PHASES", "9"))
    for mp in range(MP):
        for t in range(NT):
            emit_p0_tile(mp, t)
        tr.barrier()
        if PH < 2:
            continue

        qkv_done = [0]
        spc[0] = 0
        it_ptr = [0]
        iters = []
        for t in range(NT):
            for h in range(NH):
                iters.append((t, h, False))
            iters.append((t, None, True))

        def need_pair(t):
            while qkv_done[0] <= t:
                emit_qkv_pair(mp, qkv_done[0])
                qkv_done[0] += 2

        def drain_attn(k):
            for _ in range(k):
                if it_ptr[0] >= len(iters):
                    return
                t, h, post = iters[it_ptr[0]]
                need_pair(t)
                if post:
                    emit_attn_post(mp, t)
                else:
                    emit_attn_iter(mp, t, h)
                it_ptr[0] += 1

        def drain_rest():
            drain_attn(len(iters))

        emit_w2_load(0)
        emit_w1_dma(0, 0)
        emit_w1_dma(0, 1)
        for e in range(E):
            for tb in range(NTB):
                last = tb == NTB - 1
                e_next = (e + 1) if last else e
                if e_next < E:
                    def pf(e=e, e_next=e_next, last=last):
                        if last:
                            emit_w2_load(e_next)
                        emit_w1_dma(e_next, 0)
                        emit_w1_dma(e_next, 1)
                else:
                    pf = lambda: None
                emit_moe_unit(mp, 0, e, tb, drain=drain_attn, prefetch=pf)
        for tb in range(NTB):
            emit_b2_unit(mp, 0, tb, mT, "mT")
        drain_attn(max(0, (NT - 2) * (NH + 1) - it_ptr[0]))
        need_pair(NT - 1)
        if PH < 4:
            drain_rest()
            tr.barrier()
            continue
        if dbg:
            for t in range(NT):
                dbg_dump(dbg_d["eo1_dbg"].rearrange("(t p) d -> t p d", p=128)[mp * NT + t],
                         acc[:, t, :], [("acc", t, 0), ("acc", t, 1)])
        for t in range(NT):
            emit_p3_tile(mp, t)
            drain_attn(1)
        if PH < 5:
            drain_rest()
            tr.barrier()
            continue
        emit_w2_load(0)
        emit_w1_dma(0, 0)
        emit_w1_dma(0, 1)
        for e in range(E):
            for tb in range(NTB):
                last = tb == NTB - 1
                e_next = (e + 1) if last else e
                if e_next < E:
                    def pf(e=e, e_next=e_next, last=last):
                        if last:
                            emit_w2_load(e_next)
                        emit_w1_dma(e_next, 0)
                        emit_w1_dma(e_next, 1)
                else:
                    pf = lambda: None
                emit_moe_unit(mp, 1, e, tb, drain=drain_attn, prefetch=pf,
                              dq=(1, 1, 1, 0))
        for tb in range(NTB):
            emit_b2_unit(mp, 1, tb, mT, "mT")
        drain_rest()
        tr.barrier()
        if PH < 6:
            continue
        emit_epilogue_pre()
        for t in range(NT):
            emit_epilogue_tile(mp, t)
        tr.barrier()

    if tr.dma_cnt["out"] > 0:
        tr.stream["sync"].append(
            lambda eng, sem=tr.dma_sem["out"], v=tr.dma_cnt["out"]: eng.wait_ge(sem, v))

    with nc.Block() as block:
        @block.tensor
        def _(eng):
            for f in tr.stream["pe"]:
                f(eng)

        @block.vector
        def _(eng):
            for f in tr.stream["dve"]:
                f(eng)

        @block.scalar
        def _(eng):
            for f in tr.stream["act"]:
                f(eng)

        @block.gpsimd
        def _(eng):
            for f in tr.stream["gp"]:
                f(eng)

        @block.sync
        def _(eng):
            for f in tr.stream["sync"]:
                f(eng)

    return nc, es


def prep_inputs(inputs, TOK):
    f32 = lambda a: np.ascontiguousarray(np.asarray(a), dtype=np.float32)
    Wq, Wk, Wv = (f32(inputs["L0_attn_Wq"]), f32(inputs["L0_attn_Wk"]),
                  f32(inputs["L0_attn_Wv"]))
    wqkv = np.concatenate([Wq, Wk, Wv], axis=1)
    W1, W2 = f32(inputs["L0_ffn_W1"]), f32(inputs["L0_ffn_W2"])
    wmoe = np.empty((E, HCH, 128, 2048), dtype=np.float32)
    for e in range(E):
        for hc in range(HCH):
            w1p = W1[e][:, hc * 128:(hc + 1) * 128].reshape(DCH, 128, 128)
            wmoe[e, hc, :, 0:1024] = w1p.transpose(1, 0, 2).reshape(128, 1024)
            wmoe[e, hc, :, 1024:2048] = W2[e][hc * 128:(hc + 1) * 128, :]
    bqkvo = np.stack([f32(inputs["L0_attn_bq"]), f32(inputs["L0_attn_bk"]),
                      f32(inputs["L0_attn_bv"]), f32(inputs["L0_attn_bo"])], axis=0)
    b1 = f32(inputs["L0_ffn_b1"])
    b1p = b1.reshape(E, HCH, 128).transpose(2, 0, 1).reshape(128, E * HCH)
    cm = np.zeros((4, 16, 16), dtype=np.float32)
    for q in range(16):
        cm[:, q, :q + 1] = 1.0
    cmask = np.broadcast_to(cm.reshape(1, 1024), (128, 1024)).astype(ml_dtypes.bfloat16)
    shared = {
        "wqkv": wqkv,
        "gw": f32(inputs["L0_ffn_gate_W"]),
        "gb": f32(inputs["L0_ffn_gate_b"]).reshape(E, 1),
        "wmoe": wmoe,
        "wo": f32(inputs["L0_attn_Wo"]),
        "bqkvo": bqkvo,
        "b1p": np.ascontiguousarray(b1p),
        "b2": f32(inputs["L0_ffn_b2"]),
        "gamma": f32(inputs["L0_ffn_ln_gamma"]).reshape(1, D),
        "beta": f32(inputs["L0_ffn_ln_beta"]).reshape(1, D),
        "identf": np.eye(128, dtype=np.float32),
        "ones1b": np.ones((1, 128), dtype=ml_dtypes.bfloat16),
        "ones1f": np.ones((1, 128), dtype=np.float32),
        "cmask": np.ascontiguousarray(cmask),
        "fconst": np.tile(np.asarray([FCONST_VALS], dtype=np.float32), (128, 1)),
        "sel4": np.ascontiguousarray(
            np.repeat(np.eye(4, dtype=ml_dtypes.bfloat16), 128, axis=1)),
    }
    x = f32(inputs["L0_attn_x"]).reshape(-1, D)
    xs = [np.ascontiguousarray(x[i * TOK:(i + 1) * TOK]) for i in range(N_CORES)]
    return shared, xs


def kernel(**inputs):
    x = np.asarray(inputs["L0_attn_x"])
    n_tok = x.shape[0] * x.shape[1]
    TOK = n_tok // N_CORES
    shared, xs = prep_inputs(inputs, TOK)
    nc, es = build_program(TOK, dbg=False)
    try:
        in_maps = [dict(shared, x=xs[i]) for i in range(N_CORES)]
        res = run_bass_kernel_spmd(nc, in_maps, core_ids=list(range(N_CORES)))
    finally:
        es.close()
    out = np.concatenate([res.results[i]["out"] for i in range(N_CORES)], axis=0)
    return out.reshape(x.shape).astype(np.float32)
